# version 06 (final)
# speedup vs baseline: 1.7397x; 1.0351x over previous
import math
from contextlib import ExitStack

import numpy as np
import ml_dtypes

import concourse.bass as bass
import concourse.mybir as mybir
from concourse.bass_utils import run_bass_kernel_spmd

F32 = mybir.dt.float32
BF16 = mybir.dt.bfloat16
I32 = mybir.dt.int32
U8 = mybir.dt.uint8
AF = mybir.ActivationFunctionType
ALU = mybir.AluOpType
AX = mybir.AxisListType
BF = ml_dtypes.bfloat16

B, S, D = 2, 8192, 2048
NH, HD = 16, 128
CHUNK = 1024
TPC = 2 * CHUNK
GH, GDK, GDV = 16, 64, 128
GKD = GH * GDK
LR = 16
EPS_LN = 1e-5
EPS_GLA = 1e-5
GLN = 16.0
CC = 64
SUB = 16
II = 8192
ROPE_BASE = 10000.0
SAFE = 1.02


_ENGS = ("pe", "act", "dve", "pool", "sp")
_ENG_ALL = ("pe", "act", "dve", "pool", "sp")


class _Op:
    __slots__ = ("eng", "fn", "reads", "writes", "dma", "slot", "idx", "vc", "deps",
                 "need_inc", "inc_idx", "waits")

    def __init__(self, eng, fn, reads, writes, dma, slot=None):
        self.eng = eng
        self.fn = fn
        self.reads = reads
        self.writes = writes
        self.dma = dma
        self.slot = slot
        self.deps = []
        self.need_inc = False
        self.inc_idx = None
        self.waits = []


class Prog:
    def __init__(self):
        self.ops = []
        self.last_w = {}
        self.readers = {}
        self.float_consts = []
        self.preamble_keys = ()

    def c(self, v):
        v = float(v)
        if v not in self.float_consts:
            self.float_consts.append(v)
        return v

    def setup_consts(self, nc, ctx):
        n = max(1, len(self.float_consts))
        dr = nc.dram_tensor("fconsts", [128, n], F32, kind="ExternalInput").ap()
        sbt = ctx.enter_context(nc.sbuf_tensor("fconsts_sb", [128, n], F32))
        o = _Op("sp", lambda e: e.dma_start(out=sbt[:, :], in_=dr[:, :]),
                (), (("fconsts",),), True, "consts")
        for other in self.ops:
            other.deps.append(o)
        self.ops.insert(0, o)
        for i, v in enumerate(self.float_consts):
            nc.const_aps.register(F32, v, sbt[:, i:i + 1])
        vals = np.asarray(self.float_consts or [0.0], np.float32)
        return np.ascontiguousarray(np.tile(vals[None, :], (128, 1)))

    def op(self, eng, fn, r=(), w=(), dma=False, slot=None):
        o = _Op(eng, fn, tuple(r), tuple(w), dma, slot)
        for k in o.reads:
            p = self.last_w.get(k)
            if p is not None:
                o.deps.append(p)
            self.readers.setdefault(k, []).append(o)
        for k in o.writes:
            p = self.last_w.get(k)
            if p is not None and p is not o:
                o.deps.append(p)
            for rd in self.readers.get(k, ()):
                if rd is not o:
                    o.deps.append(rd)
            self.readers[k] = []
            self.last_w[k] = o
        self.ops.append(o)
        return o

    def dma(self, queue, out, in_, r=(), w=(), slot="misc", **kw):
        return self.op(queue, lambda e: e.dma_start(out=out, in_=in_, **kw),
                       r=r, w=w, dma=True, slot=slot)

    def mark_group(self, key):
        self.last_w[key] = self.ops[-1]

    def finalize(self, nc, ctx):
        def evt(o):
            return ("sl_" + o.slot) if o.dma else o.eng

        for o in self.ops:
            if o.dma:
                o.need_inc = True
        for o in self.ops:
            for d in o.deps:
                if d.dma or o.dma or not (d.eng == "pe" and o.eng == "pe"):
                    d.need_inc = True
        inc_count = {}
        for o in self.ops:
            e = evt(o)
            if o.need_inc:
                inc_count[e] = inc_count.get(e, 0) + 1
                o.inc_idx = inc_count[e]
        eng_clock = {e: {} for e in _ENGS}
        op_vc = {}
        running_slot = {}
        for o in self.ops:
            clk = eng_clock[o.eng]
            for d in o.deps:
                de = evt(d)
                if d.dma or not (d.eng == "pe" and o.eng == "pe" and not o.dma):
                    if d.inc_idx is None:
                        continue
                    target = running_slot.get(de, 0) if d.dma else d.inc_idx
                    dvc = op_vc[id(d)]
                    if clk.get(de, 0) < target:
                        o.waits.append((de, target))
                        for k2, v2 in dvc.items():
                            if clk.get(k2, 0) < v2:
                                clk[k2] = v2
                        clk[de] = target
            if o.waits:
                best = {}
                for sname, v in o.waits:
                    best[sname] = max(best.get(sname, 0), v)
                o.waits = sorted(best.items())
            vc = dict(clk)
            if o.need_inc:
                vc[evt(o)] = o.inc_idx
            op_vc[id(o)] = vc
            if o.need_inc and not o.dma and o.eng == "pe":
                clk[o.eng] = o.inc_idx
            if o.dma and o.need_inc:
                running_slot[evt(o)] = o.inc_idx

        term = _Op("sp", lambda e: e.nop(), (), (), False)
        term.waits = sorted((e, n) for e, n in inc_count.items() if e != "sp")
        self.ops.append(term)

        sems = {}
        used = set(evt(o) for o in self.ops if o.need_inc)
        for o in self.ops:
            for sname, _ in o.waits:
                used.add(sname)
        for e in sorted(used):
            sems[e] = ctx.enter_context(nc.semaphore("s_" + e))

        block = ctx.enter_context(nc.Block())
        by_eng = {e: [o for o in self.ops if o.eng == e] for e in _ENGS}

        def emit(eng_obj, eng_name):
            for o in by_eng[eng_name]:
                for sname, v in o.waits:
                    amt = 16 if sname.startswith("sl_") else 1
                    eng_obj.wait_ge(sems[sname], v * amt)
                inst = o.fn(eng_obj)
                if o.need_inc:
                    amt = 16 if o.dma else 1
                    inst.then_inc(sems[evt(o)], amt)

        if by_eng["sp"]:
            block.sync(lambda e: emit(e, "sp"))
        if by_eng["act"]:
            block.scalar(lambda e: emit(e, "act"))
        if by_eng["dve"]:
            block.vector(lambda e: emit(e, "dve"))
        if by_eng["pool"]:
            block.gpsimd(lambda e: emit(e, "pool"))
        if by_eng["pe"]:
            block.tensor(lambda e: emit(e, "pe"))
        return sems



def _perm_eo():
    p = []
    for h in range(NH):
        base = h * HD
        p += [base + i for i in range(0, HD, 2)]
        p += [base + i for i in range(1, HD, 2)]
    return np.asarray(p, dtype=np.int64)


def _own_rows(p):
    return (list(range(CHUNK * p, CHUNK * (p + 1)))
            + list(range(CHUNK * (7 - p), CHUNK * (8 - p))))


def host_stage(inp):
    st = {}
    perm = _perm_eo()
    x0 = np.asarray(inp["L0_attn_a_x"], dtype=np.float32)

    Wq = np.asarray(inp["L0_attn_a_Wq"], np.float32)
    Wk = np.asarray(inp["L0_attn_a_Wk"], np.float32)
    Wv = np.asarray(inp["L0_attn_a_Wv"], np.float32)
    Wo = np.asarray(inp["L0_attn_a_Wo"], np.float32)
    st["wqt"] = np.ascontiguousarray(Wq.T[:, perm])
    st["wkt"] = np.ascontiguousarray(Wk.T[:, perm])
    st["wvt"] = np.ascontiguousarray(Wv.T)
    st["wot"] = np.ascontiguousarray(Wo.T)
    st["wos"] = np.ascontiguousarray(
        st["wot"].reshape(16, 128, 8, 256).transpose(1, 2, 0, 3).reshape(128, 32768))
    st["bq"] = np.ascontiguousarray(np.asarray(inp["L0_attn_a_bq"], np.float32)[perm].reshape(16, 128).T)
    st["bk"] = np.ascontiguousarray(np.asarray(inp["L0_attn_a_bk"], np.float32)[perm].reshape(16, 128).T)
    st["bv"] = np.ascontiguousarray(np.asarray(inp["L0_attn_a_bv"], np.float32).reshape(1, D))
    st["bo"] = np.ascontiguousarray(np.asarray(inp["L0_attn_a_bo"], np.float32).reshape(16, 128).T)

    half = HD // 2
    inv = (1.0 / ROPE_BASE ** (np.arange(half, dtype=np.float64) / half))
    t = np.arange(S, dtype=np.float64)
    fr = np.outer(inv, t)
    cos = np.concatenate([np.cos(fr), np.cos(fr)], 0).astype(np.float32)
    sin = np.concatenate([np.sin(fr), np.sin(fr)], 0).astype(np.float32)

    st["wqb"] = np.ascontiguousarray(np.asarray(inp["L0_attn_b_Wq"], np.float32))
    st["wkb"] = np.ascontiguousarray(np.asarray(inp["L0_attn_b_Wk"], np.float32))
    st["wvb"] = np.ascontiguousarray(np.asarray(inp["L0_attn_b_Wv"], np.float32))
    st["wgb"] = np.ascontiguousarray(np.asarray(inp["L0_attn_b_Wg"], np.float32))
    st["wgbs"] = np.ascontiguousarray(
        st["wgb"].reshape(16, 128, 8, 256).transpose(1, 2, 0, 3).reshape(128, 32768))
    st["wgk1"] = np.ascontiguousarray(np.asarray(inp["L0_attn_b_Wgk1"], np.float32))
    st["wgk2"] = np.ascontiguousarray(np.asarray(inp["L0_attn_b_Wgk2"], np.float32))
    st["bgk2"] = np.ascontiguousarray(np.asarray(inp["L0_attn_b_bgk2"], np.float32).reshape(1, GKD))
    st["nw"] = np.ascontiguousarray(np.asarray(inp["L0_attn_b_norm_w"], np.float32).reshape(GDV, 1))
    st["wob"] = np.ascontiguousarray(np.asarray(inp["L0_attn_b_Wo"], np.float32))

    st["wgt"] = np.ascontiguousarray(np.asarray(inp["L0_ffn_gate_w"], np.uint8).T)
    st["wut"] = np.ascontiguousarray(np.asarray(inp["L0_ffn_up_w"], np.uint8).T)
    st["wdt"] = np.ascontiguousarray(np.asarray(inp["L0_ffn_down_w"], np.uint8).T)
    st["wgs"] = np.ascontiguousarray(
        st["wgt"].reshape(16, 128, 32, 256).transpose(1, 2, 0, 3).reshape(128, 131072))
    st["wus"] = np.ascontiguousarray(
        st["wut"].reshape(16, 128, 32, 256).transpose(1, 2, 0, 3).reshape(128, 131072))
    st["wds"] = np.ascontiguousarray(
        st["wdt"].reshape(4, 16, 128, 8, 256).transpose(2, 0, 3, 1, 4).reshape(128, 131072))
    st["sg"] = np.ascontiguousarray(np.asarray(inp["L0_ffn_gate_scale"], np.float32).reshape(64, 128).T)
    st["su"] = np.ascontiguousarray(np.asarray(inp["L0_ffn_up_scale"], np.float32).reshape(64, 128).T)
    st["sd"] = np.ascontiguousarray(np.asarray(inp["L0_ffn_down_scale"], np.float32).reshape(16, 128).T)
    st["sdr"] = np.ascontiguousarray(np.asarray(inp["L0_ffn_down_scale"], np.float32).reshape(1, 2048))

    st["eye"] = np.eye(128, dtype=np.float32)
    km = np.arange(128)[:, None]
    qm = np.arange(512)[None, :]
    st["masks"] = np.ascontiguousarray(np.concatenate(
        [((o + km) <= qm).astype(np.float32) for o in (0, 128, 256, 384)], axis=1))

    i = np.arange(CC)
    tri_inc = (i[:, None] <= i[None, :]).astype(np.float32)
    bnd = (i[:, None] <= (i[None, :] // SUB) * SUB - 1).astype(np.float32)
    upp = (i[:, None] > i[None, :]).astype(np.float32)

    def blkdiag(m):
        z = np.zeros((128, 128), np.float32)
        z[:CC, :CC] = m
        z[CC:, CC:] = m
        return z / GLN
    st["tri"] = np.ascontiguousarray(np.concatenate(
        [blkdiag(tri_inc), blkdiag(bnd), blkdiag(upp)], axis=1))

    r = np.arange(CC)[:, None]
    smask = np.zeros((CC, CC), np.float32)
    for u in range(4):
        tt = (u * SUB + np.arange(SUB))[None, :]
        smask[:, u * SUB:(u + 1) * SUB] = (r <= tt).astype(np.float32)
    st["smask"] = np.ascontiguousarray(smask)

    st["x0c"], st["cosc"], st["sinc"] = [], [], []
    for c in range(8):
        b, p = divmod(c, 4)
        rows = _own_rows(p)
        st["x0c"].append(np.ascontiguousarray(x0[b, rows, :]))
        st["cosc"].append(np.ascontiguousarray(cos[:, rows]))
        st["sinc"].append(np.ascontiguousarray(sin[:, rows]))
    return st


NPAST = ((24, 28), (56, 60))
PENC = -30.0
PENPAD = -1e9


def stage_L2(st, l1res):
    KF = [None, None]
    VF = [None, None]
    for b in range(2):
        K = np.zeros((D, S), BF)
        V = np.zeros((S, D), BF)
        for p in range(4):
            c = 4 * b + p
            rows = _own_rows(p)
            K[:, rows] = np.asarray(l1res[c]["k"])
            V[rows, :] = np.asarray(l1res[c]["v"])
        KF[b], VF[b] = K, V
    out = []
    for c in range(8):
        b, p = divmod(c, 4)
        d = {}
        for slot, (a0, ncols, npref_pad) in enumerate(
                ((CHUNK * p, 4096, 3072), (CHUNK * (7 - p), 8192, 7168))):
            k_st = np.zeros((D, ncols), BF)
            v_st = np.zeros((ncols, D), BF)
            k_st[:, :a0] = KF[b][:, :a0]
            v_st[:a0, :] = VF[b][:a0, :]
            k_st[:, npref_pad:npref_pad + 1024] = KF[b][:, a0:a0 + 1024]
            v_st[npref_pad:npref_pad + 1024, :] = VF[b][a0:a0 + 1024, :]
            d["kpA" if slot == 0 else "kpB"] = np.ascontiguousarray(k_st)
            d["vpA" if slot == 0 else "vpB"] = np.ascontiguousarray(v_st)
        pen = np.full((128, 96), PENC, np.float32)
        for slot, (a0, base, nblk) in enumerate(((CHUNK * p, 0, 32), (CHUNK * (7 - p), 32, 64))):
            for blk in range(nblk):
                k0 = blk * 128
                kabs = np.arange(k0, k0 + 128)
                ncols = 4096 if slot == 0 else 8192
                npp = 3072 if slot == 0 else 7168
                valid = (kabs < a0) | ((kabs >= npp) & (kabs < npp + 1024))
                pen[:, base + blk] = np.where(valid, PENC, PENPAD)
        d["pen"] = np.ascontiguousarray(pen)
        d["q"] = np.asarray(l1res[c]["q"])
        out.append(d)
    return out



def emit_ln_fm_from_tiles(P, xin_tiles, tcols, stat, ones_pe, psums, out_tiles,
                          key, out_bf=True):
    raise NotImplementedError


def build_L1(core_p, stages=99, micro=99):
    nc = bass.Bass("TRN2", target_bir_lowering=False)
    x0c = nc.dram_tensor("x0c", [TPC, D], F32, kind="ExternalInput").ap()
    wqt = nc.dram_tensor("wqt", [D, D], F32, kind="ExternalInput").ap()
    wkt = nc.dram_tensor("wkt", [D, D], F32, kind="ExternalInput").ap()
    wvt = nc.dram_tensor("wvt", [D, D], F32, kind="ExternalInput").ap()
    bq = nc.dram_tensor("bq", [128, 16], F32, kind="ExternalInput").ap()
    bk = nc.dram_tensor("bk", [128, 16], F32, kind="ExternalInput").ap()
    bv = nc.dram_tensor("bv", [1, D], F32, kind="ExternalInput").ap()
    cosc = nc.dram_tensor("cosc", [HD, TPC], F32, kind="ExternalInput").ap()
    sinc = nc.dram_tensor("sinc", [HD, TPC], F32, kind="ExternalInput").ap()
    eye = nc.dram_tensor("eye", [128, 128], F32, kind="ExternalInput").ap()

    q_o = nc.dram_tensor("q", [D, TPC], BF16, kind="ExternalOutput").ap()
    k_o = nc.dram_tensor("k", [D, TPC], BF16, kind="ExternalOutput").ap()
    v_o = nc.dram_tensor("v", [TPC, D], BF16, kind="ExternalOutput").ap()
    kn_o = nc.dram_tensor("kn", [1, NH], F32, kind="ExternalOutput").ap()
    xn_o = nc.dram_tensor("xn", [D, TPC], BF16, kind="ExternalOutput").ap()

    NT = TPC // 128
    ctx = ExitStack()
    sb = lambda *a: ctx.enter_context(nc.sbuf_tensor(*a))
    pst = lambda *a: ctx.enter_context(nc.psum_tensor(*a))
    P = Prog()

    xtm = [sb(f"xtm{i}", [128, D], F32) for i in range(2)]
    xnt = [sb(f"xnt{i}", [128, D], BF16) for i in range(2)]
    xfm = [sb(f"xfm{j}", [128, TPC], BF16) for j in range(16)]
    sqscr = [sb(f"sqscr{i}", [128, D], F32) for i in range(2)]
    stat = sb("stat", [128, 6 * NT], F32)
    eye_f = sb("eye_f", [128, 128], F32)
    eye_b = sb("eye_b", [128, 128], BF16)
    cos_s = sb("cos_s", [128, TPC], F32)
    sin_s = sb("sin_s", [128, TPC], F32)
    bq_c = sb("bq_c", [128, 16], F32)
    bk_c = sb("bk_c", [128, 16], F32)
    bv_f = sb("bv_f", [1, D], F32)
    bv_r = sb("bv_r", [1, D], BF16)
    ones_r = sb("ones_r", [1, 128], BF16)
    ones_c = sb("ones_c", [128, 1], BF16)
    wbuf = [sb(f"wbuf{i}", [128, 512], F32) for i in range(4)]
    wgrp = [sb(f"wgrp{i}", [128, 512], BF16) for i in range(16)]
    wgrp2 = [sb(f"wgrp2_{i}", [128, 512], BF16) for i in range(16)]
    qpre = [sb(f"qpre{i}", [128, 512], F32) for i in range(3)]
    qrot = [sb(f"qrot{i}", [128, 512], F32) for i in range(3)]
    qbf = [sb(f"qbf{i}", [128, 512], BF16) for i in range(3)]
    ksq = [sb(f"ksq{i}", [128, 512], BF16) for i in range(2)]
    knmax = sb("knmax", [1, NH], F32)
    knrow = sb("knrow", [1, NH * 4], F32)

    pt = [pst(f"pt{i}", [128, 128], BF16) for i in range(4)]
    pp = [pst(f"pp{i}", [128, 512], F32) for i in range(3)]
    pkn = pst("pkn", [1, 512], F32)

    P.dma("sp", cos_s[:, :], cosc[:, :], w=[("cos",)], slot="ccos")
    P.dma("sp", sin_s[:, :], sinc[:, :], w=[("sin",)], slot="csin")
    P.dma("sp", eye_f[:, :], eye[:, :], w=[("eyef",)], slot="ceye")
    P.dma("sp", bq_c[:, :], bq[:, :], w=[("bqc",)], slot="cbq")
    P.dma("sp", bk_c[:, :], bk[:, :], w=[("bkc",)], slot="cbk")
    P.dma("sp", bv_f[:, :], bv[:, :], w=[("bvf",)], slot="cbv")
    P.op("act", lambda e: e.activation(out=eye_b[:, :], in_=eye_f[:, :], func=AF.Copy),
         r=[("eyef",)], w=[("eyeb",)])
    P.op("act", lambda e: e.activation(out=bv_r[:, :], in_=bv_f[:, :], func=AF.Copy),
         r=[("bvf",)], w=[("bvr",)])
    P.op("dve", lambda e: e.memset(ones_r[:, :], 1.0), w=[("onesr",)])
    P.op("dve", lambda e: e.memset(ones_c[:, :], 1.0), w=[("onesc",)])

    eps_c = P.c(EPS_LN)
    for i in range(NT if micro >= 1 else 0):
        c0 = 6 * i
        sq = sqscr[i % 2]
        xt = xtm[i % 2]
        P.dma("sp", xt[:, :], x0c[i * 128:(i + 1) * 128, :], w=[("xtm", i % 2)],
              slot=f"xtm{i % 2}")
        P.op("act", lambda e, sq=sq, xt=xt, c0=c0: e.activation(
            out=sq[:, :], in_=xt[:, :], func=AF.Square, accum_out=stat[:, c0 + 1:c0 + 2]),
            r=[("xtm", i % 2)], w=[("sq", i % 2), ("st", i, 1)])
        P.op("dve", lambda e, xt=xt, c0=c0: e.reduce_sum(
            stat[:, c0:c0 + 1], xt[:, :], axis=AX.X),
            r=[("xtm", i % 2)], w=[("st", i, 0)])
        P.op("dve", lambda e, c0=c0: e.tensor_scalar(
            stat[:, c0 + 2:c0 + 3], stat[:, c0:c0 + 1], 1.0 / D, None, ALU.mult),
            r=[("st", i, 0)], w=[("st", i, 2)])
        P.op("dve", lambda e, c0=c0: e.tensor_scalar(
            stat[:, c0 + 3:c0 + 4], stat[:, c0 + 2:c0 + 3], stat[:, c0 + 2:c0 + 3],
            None, ALU.mult),
            r=[("st", i, 2)], w=[("st", i, 3)])
        P.op("dve", lambda e, c0=c0: e.tensor_scalar(
            stat[:, c0 + 4:c0 + 5], stat[:, c0 + 1:c0 + 2], 1.0 / D, stat[:, c0 + 3:c0 + 4],
            ALU.mult, ALU.subtract),
            r=[("st", i, 1), ("st", i, 3)], w=[("st", i, 4)])
        P.op("act", lambda e, c0=c0: e.activation(
            out=stat[:, c0 + 5:c0 + 6], in_=stat[:, c0 + 4:c0 + 5], func=AF.Sqrt,
            bias=eps_c),
            r=[("st", i, 4)], w=[("st", i, 5)])
        P.op("dve", lambda e, c0=c0: e.reciprocal(
            stat[:, c0 + 4:c0 + 5], stat[:, c0 + 5:c0 + 6]),
            r=[("st", i, 5)], w=[("st", i, 4)])
        P.op("dve", lambda e, xt=xt, i=i, c0=c0: e.tensor_scalar(
            xnt[i % 2][:, :], xt[:, :], stat[:, c0 + 2:c0 + 3], stat[:, c0 + 4:c0 + 5],
            ALU.subtract, ALU.mult),
            r=[("xtm", i % 2), ("st", i, 2), ("st", i, 4)], w=[("xnt", i % 2)])

        for j4 in range(4 if micro >= 2 else 0):
            for jj in range(4):
                j = 4 * j4 + jj
                bank = pt[jj]
                P.op("pe", lambda e, i=i, j=j, bank=bank: e.transpose(
                    bank[:, :], xnt[i % 2][:, j * 128:(j + 1) * 128],
                    eye_b[:, :]),
                    r=[("xnt", i % 2), ("eyeb",)], w=[("pt", jj)])
            for jj in range(4):
                j = 4 * j4 + jj
                bank = pt[jj]
                P.op("dve", lambda e, i=i, j=j, bank=bank: e.tensor_copy(
                    xfm[j][:, i * 128:(i + 1) * 128], bank[:, :]),
                    r=[("pt", jj)], w=[("xfm", j, i // 4)])

    def w_load(wsrc, mt, arena, akey):
        for hc in range(16):
            wb = wbuf[hc % 4]
            P.dma("sp", wb[:, :], wsrc[hc * 128:(hc + 1) * 128, mt * 512:(mt + 1) * 512],
                  r=[("wbuf", hc % 4)], w=[("wbuf", hc % 4)], slot=f"wbuf{hc % 4}")
            P.op("act", lambda e, wb=wb, hc=hc, arena=arena: e.activation(
                out=arena[hc][:, :], in_=wb[:, :], func=AF.Copy),
                r=[("wbuf", hc % 4)] + [(akey, hc)], w=[(akey, hc)])

    nmm = 0

    def emit_qk(kind, mt, arena, akey):
        nonlocal nmm
        bias_sb, bias_key = (bq_c, "bqc") if kind == "q" else (bk_c, "bkc")
        out_dram = q_o if kind == "q" else k_o
        scale = (1.0 / math.sqrt(HD)) if kind == "q" else 1.0
        do_norm = kind == "k"
        for m in range(4):
            mc = mt * 4 + m
            head = mc
            for tt in range(4):
                bank = pp[nmm % 3]
                nmm += 1
                for hc in range(16):
                    P.op("pe", lambda e, bank=bank, arena=arena, hc=hc, m=m, tt=tt: e.matmul(
                        bank[:, :], arena[hc][:, m * 128:(m + 1) * 128],
                        xfm[hc][:, tt * 512:(tt + 1) * 512],
                        start=(hc == 0), stop=(hc == 15)),
                        r=[(akey, hc), ("xfm", hc, tt)], w=[("pp", id(bank))])
                qb = qpre[nmm % 3]
                rb = qrot[nmm % 3]
                ob = qbf[nmm % 3]
                P.op("dve", lambda e, bank=bank, qb=qb, mc=mc, scale=scale,
                     bias_sb=bias_sb: e.tensor_scalar(
                    qb[:, :], bank[:, :], bias_sb[:, mc:mc + 1], scale, ALU.add, ALU.mult),
                    r=[("pp", id(bank)), (bias_key,)], w=[("qpre", id(qb))])
                P.dma("pool", rb[0:64, :], qb[64:128, :],
                      r=[("qpre", id(qb))], w=[("qrot", id(rb), 0)],
                      slot=f"qrot{nmm % 3}a")
                P.dma("pool", rb[64:128, :], qb[0:64, :],
                      r=[("qpre", id(qb))], w=[("qrot", id(rb), 1)],
                      slot=f"qrot{nmm % 3}b")
                P.op("pool", lambda e, rb=rb, tt=tt: e.tensor_mul(
                    rb[0:64, :], rb[0:64, :], sin_s[0:64, tt * 512:(tt + 1) * 512]),
                    r=[("qrot", id(rb), 0), ("sin",)], w=[("qrot", id(rb), 0)])
                P.op("pool", lambda e, rb=rb, tt=tt: e.tensor_mul(
                    rb[64:128, :], rb[64:128, :], sin_s[64:128, tt * 512:(tt + 1) * 512]),
                    r=[("qrot", id(rb), 1), ("sin",)], w=[("qrot", id(rb), 1)])
                P.op("dve", lambda e, qb=qb, tt=tt: e.tensor_mul(
                    qb[:, :], qb[:, :], cos_s[:, tt * 512:(tt + 1) * 512]),
                    r=[("qpre", id(qb)), ("cos",)], w=[("qpre", id(qb))])
                P.op("dve", lambda e, qb=qb, rb=rb, ob=ob: e.tensor_sub(
                    ob[0:64, :], qb[0:64, :], rb[0:64, :]),
                    r=[("qpre", id(qb)), ("qrot", id(rb), 0)], w=[("qbf", id(ob), 0)])
                P.op("dve", lambda e, qb=qb, rb=rb, ob=ob: e.tensor_add(
                    ob[64:128, :], qb[64:128, :], rb[64:128, :]),
                    r=[("qpre", id(qb)), ("qrot", id(rb), 1)], w=[("qbf", id(ob), 1)])
                P.dma("pool", out_dram[mc * 128:(mc + 1) * 128, tt * 512:(tt + 1) * 512],
                      ob[:, :], r=[("qbf", id(ob), 0), ("qbf", id(ob), 1)],
                      w=[("out", kind, mc, tt)], slot=f"st{nmm % 3}")

    def emit_v(kind, dt, arena, akey):
        for ti in range(NT):
            bank = pp[ti % 3]
            for hc in range(16):
                P.op("pe", lambda e, bank=bank, arena=arena, hc=hc, ti=ti: e.matmul(
                    bank[:, :], xfm[hc][:, ti * 128:(ti + 1) * 128], arena[hc][:, :],
                    start=(hc == 0), stop=False),
                    r=[(akey, hc), ("xfm", hc, ti // 4)], w=[("pp", id(bank))])
            P.op("pe", lambda e, bank=bank, dt=dt: e.matmul(
                bank[:, :], ones_r[:, :], bv_r[:, dt * 512:(dt + 1) * 512],
                start=False, stop=True),
                r=[("onesr",), ("bvr",)], w=[("pp", id(bank))])
            ob = qbf[ti % 3]
            P.op("dve", lambda e, bank=bank, ob=ob: e.tensor_copy(
                ob[:, :], bank[:, :]),
                r=[("pp", id(bank))], w=[("qbf", id(ob), 0), ("qbf", id(ob), 1)])
            P.dma("pool", v_o[ti * 128:(ti + 1) * 128, dt * 512:(dt + 1) * 512], ob[:, :],
                  r=[("qbf", id(ob), 0), ("qbf", id(ob), 1)],
                  w=[("out", "v", dt, ti)], slot=f"st{ti % 3}")

    segs = []
    if stages >= 2:
        segs += [(wqt, mt, "q") for mt in range(4)]
    if stages >= 3:
        segs += [(wkt, mt, "k") for mt in range(4)]
    if stages >= 4:
        segs += [(wvt, dt, "v") for dt in range(4)]
    arenas = [(wgrp, "wgA"), (wgrp2, "wgB")]
    if segs:
        w_load(segs[0][0], segs[0][1], *arenas[0])
    for s, (wsrc, mt, kind) in enumerate(segs):
        arena, akey = arenas[s % 2]
        if s + 1 < len(segs):
            w_load(segs[s + 1][0], segs[s + 1][1], *arenas[(s + 1) % 2])
        if kind == "v":
            emit_v(kind, mt, arena, akey)
        else:
            emit_qk(kind, mt, arena, akey)

    if stages >= 3:
        P.op("dve", lambda e: e.memset(knmax[:, :], 0.0), w=[("knm",)])
        P.dma("pool", kn_o[:, :], knmax[:, :], r=[("knm",)], w=[("out", "kn")],
              slot="stkn")

    del xn_o

    out_keys = [k for k in P.last_w if k and k[0] == "out"]
    P.op("sp", lambda e: e.nop(), r=out_keys)

    fc = P.setup_consts(nc, ctx)
    sems = P.finalize(nc, ctx)
    ctx.close()
    return nc, fc


def run_L1(st, stages=99, micro=99):
    nc, fc = build_L1(0, stages, micro)
    in_maps = []
    for c in range(8):
        in_maps.append({
            "x0c": st["x0c"][c], "wqt": st["wqt"], "wkt": st["wkt"], "wvt": st["wvt"],
            "bq": st["bq"], "bk": st["bk"], "bv": st["bv"],
            "cosc": st["cosc"][c], "sinc": st["sinc"][c], "eye": st["eye"],
            "fconsts": fc,
        })
    res = run_bass_kernel_spmd(nc, in_maps, core_ids=list(range(8)))
    return res.results



def build_L2():
    nc = bass.Bass("TRN2", target_bir_lowering=False)
    q_d = nc.dram_tensor("q", [D, TPC], BF16, kind="ExternalInput").ap()
    kpA = nc.dram_tensor("kpA", [D, 4096], BF16, kind="ExternalInput").ap()
    kpB = nc.dram_tensor("kpB", [D, 8192], BF16, kind="ExternalInput").ap()
    vpA = nc.dram_tensor("vpA", [4096, D], BF16, kind="ExternalInput").ap()
    vpB = nc.dram_tensor("vpB", [8192, D], BF16, kind="ExternalInput").ap()
    pen_d = nc.dram_tensor("pen", [128, 96], F32, kind="ExternalInput").ap()
    x0c = nc.dram_tensor("x0c", [TPC, D], F32, kind="ExternalInput").ap()
    wos = nc.dram_tensor("wos", [128, 32768], F32, kind="ExternalInput").ap()
    bo_d = nc.dram_tensor("bo", [128, 16], F32, kind="ExternalInput").ap()
    masks_d = nc.dram_tensor("masks", [128, 2048], F32, kind="ExternalInput").ap()
    wgs = nc.dram_tensor("wgs", [128, 131072], U8, kind="ExternalInput").ap()
    wus = nc.dram_tensor("wus", [128, 131072], U8, kind="ExternalInput").ap()
    wds = nc.dram_tensor("wds", [128, 131072], U8, kind="ExternalInput").ap()
    sg_d = nc.dram_tensor("sg", [128, 64], F32, kind="ExternalInput").ap()
    su_d = nc.dram_tensor("su", [128, 64], F32, kind="ExternalInput").ap()
    sd_d = nc.dram_tensor("sd", [128, 16], F32, kind="ExternalInput").ap()
    eye_d = nc.dram_tensor("eye", [128, 128], F32, kind="ExternalInput").ap()

    x2_o = nc.dram_tensor("x2", [D, TPC], F32, kind="ExternalOutput").ap()
    x2n_o = nc.dram_tensor("x2n", [D, TPC], BF16, kind="ExternalOutput").ap()
    x0fm = nc.dram_tensor("x0fm", [D, TPC], F32, kind="Internal").ap()
    ofm = nc.dram_tensor("ofm", [D, TPC], BF16, kind="Internal").ap()

    ctx = ExitStack()
    sb = lambda *a: ctx.enter_context(nc.sbuf_tensor(*a))
    pst = lambda *a: ctx.enter_context(nc.psum_tensor(*a))
    P = Prog()

    qar = [sb(f"qar{j}", [128, TPC], BF16) for j in range(16)]
    kvb = [sb(f"kvb{j}", [128, 8192], BF16) for j in range(4)]
    SCRF = sb("scrf", [128, 8192], F32)
    SCRB = SCRF[:, :].bitcast(BF16)
    wgrp = [sb(f"wgrp{j}", [128, 512], BF16) for j in range(16)]
    wstg = [sb(f"wstg{j}", [128, 512], F32) for j in range(2)]
    wstgi = [wstg[j][:, :].bitcast(U8) for j in range(2)]
    masks_b = sb("masks_b", [128, 2048], BF16)
    pen_s = sb("pen_s", [128, 96], F32)
    xstg = [sb("xstg0", [128, 1024], F32), sb("xstg1", [128, 512], F32)]
    eye_f = xstg[1][:, 0:128]
    pbf = [sb(f"pbf{j}", [128, 512], BF16) for j in range(4)]
    obf = [sb(f"obf{j}", [128, 512], BF16) for j in range(4)]
    lrow = sb("lrow", [1, 512], F32)
    rrow = sb("rrow", [1, 512], F32)
    onec_b = sb("onec_b", [128, 1], BF16)
    oner_f = sb("oner_f", [1, 128], F32)
    onec_f = sb("onec_f", [128, 1], F32)
    bo_c = sb("bo_c", [128, 16], F32)
    sg_c = sb("sg_c", [128, 64], F32)
    su_c = sb("su_c", [128, 64], F32)
    sd_c = sb("sd_c", [128, 16], F32)
    sqt = [sb(f"sqt{j}", [128, 512], F32) for j in range(2)]
    gst = pbf
    ust = obf

    pbig = [pst(f"pbig{j}", [128, 512], F32) for j in range(5)]
    pmb = pst("pmb", [128, 512], F32)
    prow = pst("prow", [1, 512], F32)
    prow2 = pst("prow2", [1, 512], F32)

    def PS(t):
        return ("PS", id(t))

    def SR(lo, hi):
        return [("scrf", r) for r in range(lo, hi)]

    def x1t(oc):
        return SCRF[:, oc * 512:(oc + 1) * 512]

    def ugrp(j):
        return SCRB[:, j * 512:(j + 1) * 512]

    def dtmp(j):
        return SCRF[:, 4096 + j * 512:4096 + (j + 1) * 512]

    def x1n(j):
        return kvb[j // 4][:, (j % 4) * 2048:(j % 4 + 1) * 2048]

    eps_c = P.c(EPS_LN)

    P.dma("sp", pen_s[:, :], pen_d[:, :], w=[("pen",)], slot="cp")
    P.dma("sp", eye_f, eye_d[:, :], w=[("eyef",), ("xstg", 1)], slot="ce")
    P.dma("sp", bo_c[:, :], bo_d[:, :], w=[("boc",)], slot="cb")
    P.dma("sp", sg_c[:, :], sg_d[:, :], w=[("sgc",)], slot="cs1")
    P.dma("sp", su_c[:, :], su_d[:, :], w=[("suc",)], slot="cs2")
    P.dma("sp", sd_c[:, :], sd_d[:, :], w=[("sdc",)], slot="cs3")
    for mh in range(2):
        P.dma("sp", xstg[0][:, :], masks_d[:, mh * 1024:(mh + 1) * 1024],
              r=[("xstg", 0)], w=[("xstg", 0)], slot="xst0")
        P.op("act", lambda e, mh=mh: e.activation(
            out=masks_b[:, mh * 1024:(mh + 1) * 1024], in_=xstg[0][:, :], func=AF.Copy),
            r=[("xstg", 0)], w=[("maskb",)])
    P.op("dve", lambda e: e.memset(onec_b[:, :], 1.0), w=[("onecb",)])
    P.op("dve", lambda e: e.memset(oner_f[:, :], 1.0), w=[("onerf",)])
    P.op("dve", lambda e: e.memset(onec_f[:, :], 1.0), w=[("onecf",)])

    for j in range(16):
        P.dma("pool", qar[j][:, :], q_d[j * 128:(j + 1) * 128, :],
              w=[("qar", j, q4) for q4 in range(4)], slot="qload")

    kcols = (4096, 8192)
    kdram = (kpA, kpB)
    vdram = (vpA, vpB)
    nit = 0
    for slot in range(2):
        ncol = kcols[slot]
        for h in range(16):
            kb_ = kvb[nit % 2]
            vb_ = kvb[2 + nit % 2]
            kslot, vslot = nit % 2, 2 + nit % 2
            nit += 1
            P.dma("sp", kb_[:, 0:ncol], kdram[slot][h * 128:(h + 1) * 128, :],
                  w=[("kvb", kslot)], slot=f"kv{kslot}")
            P.dma("sp", vb_[:, 0:ncol].rearrange("k (a d) -> k a d", d=128),
                  vdram[slot][:, h * 128:(h + 1) * 128].rearrange("(a k) d -> k a d", k=128),
                  w=[("kvb", vslot)], slot=f"kv{vslot}")
            for j in range(2):
                qcol = slot * 1024 + j * 512
                npast = NPAST[slot][j]
                dgbase = (3072 if slot == 0 else 7168) + 512 * j
                nblk = npast + 4
                po = pbig[2 + j]

                def s_mm(blk):
                    diag = blk >= npast
                    k0 = (blk - npast) * 128 + dgbase if diag else blk * 128
                    ps = pbig[(0, 1, 4)[blk % 3]]
                    P.op("pe", lambda e, kb_=kb_, k0=k0, h=h, qcol=qcol, ps=ps: e.matmul(
                        ps[:, :], kb_[:, k0:k0 + 128], qar[h][:, qcol:qcol + 512],
                        start=True, stop=True),
                        r=[("kvb", kslot), ("qar", h, qcol // 512)], w=[PS(ps)])

                s_mm(0)
                s_mm(1)
                for blk in range(nblk):
                    diag = blk >= npast
                    k0 = (blk - npast) * 128 + dgbase if diag else blk * 128
                    pcol = (slot * 32) + k0 // 128
                    ps = pbig[(0, 1, 4)[blk % 3]]
                    pb = pbf[blk % 4]
                    P.op("act", lambda e, ps=ps, pb=pb, pcol=pcol: e.activation(
                        out=pb[:, :], in_=ps[:, :], func=AF.Exp,
                        bias=pen_s[:, pcol:pcol + 1]),
                        r=[PS(ps), ("pen",)], w=[("pbf", id(pb))])
                    if diag:
                        moff = (blk - npast) * 512
                        P.op("dve", lambda e, pb=pb, moff=moff: e.tensor_mul(
                            pb[:, :], pb[:, :], masks_b[:, moff:moff + 512]),
                            r=[("pbf", id(pb)), ("maskb",)], w=[("pbf", id(pb))])
                    if blk + 2 < nblk:
                        s_mm(blk + 2)
                    P.op("pe", lambda e, pb=pb, blk=blk, nblk=nblk: e.matmul(
                        prow[:, :], onec_b[:, :], pb[:, :],
                        start=(blk == 0), stop=(blk == nblk - 1)),
                        r=[("pbf", id(pb)), ("onecb",)], w=[PS(prow)])
                    P.op("pe", lambda e, vb_=vb_, k0=k0, pb=pb, po=po, blk=blk,
                         nblk=nblk: e.matmul(
                        po[:, :], vb_[:, k0:k0 + 128], pb[:, :],
                        start=(blk == 0), stop=(blk == nblk - 1)),
                        r=[("kvb", vslot), ("pbf", id(pb))], w=[PS(po)])
                P.op("dve", lambda e: e.tensor_copy(lrow[:, :], prow[:, :]),
                     r=[PS(prow)], w=[("lrow",)])
                P.op("dve", lambda e: e.reciprocal(lrow[:, :], lrow[:, :]),
                     r=[("lrow",)], w=[("lrow",)])
                P.op("pe", lambda e: e.matmul(
                    pmb[:, :], oner_f[:, :], lrow[:, :], start=True, stop=True),
                    r=[("lrow",), ("onerf",)], w=[PS(pmb)])
                rsb = sqt[1]
                P.op("dve", lambda e, rsb=rsb: e.tensor_copy(rsb[:, :], pmb[:, :]),
                     r=[PS(pmb)], w=[("sqt", 1)])
                P.op("dve", lambda e, po=po, rsb=rsb, h=h, qcol=qcol: e.tensor_mul(
                    qar[h][:, qcol:qcol + 512], po[:, :], rsb[:, :]),
                    r=[PS(po), ("sqt", 1), ("qar", h, qcol // 512)],
                    w=[("qar", h, qcol // 512)])

    for i in range(16):
        for hf in range(2):
            n = 2 * i + hf
            p = n % 2
            xs_ = SCRF[:, 6144 + p * 1024:7168 + p * 1024]
            stk = [("scrf", 12 + 2 * p), ("scrf", 13 + 2 * p)]
            P.dma("pool", xs_, x0c[i * 128:(i + 1) * 128, hf * 1024:(hf + 1) * 1024],
                  r=stk, w=stk, slot=f"x0l{p}")
            for j4 in range(2):
                for jj in range(4):
                    j = 8 * hf + 4 * j4 + jj
                    bank = pbig[jj]
                    P.op("pe", lambda e, xs_=xs_, j=j, jj=jj, j4=j4, bank=bank: e.transpose(
                        bank[:, 0:128], xs_[:, (4 * j4 + jj) * 128:(4 * j4 + jj + 1) * 128],
                        eye_f),
                        r=stk + [("eyef",)], w=[PS(bank)])
                for jj in range(4):
                    j = 8 * hf + 4 * j4 + jj
                    bank = pbig[jj]
                    ts_ = sqt[jj % 2][:, (jj // 2) * 128:(jj // 2 + 1) * 128]
                    P.op("dve", lambda e, bank=bank, ts_=ts_, jj=jj: e.tensor_copy(
                        ts_, bank[:, 0:128]),
                        r=[PS(bank), ("sqt", jj % 2)], w=[("sqt", jj % 2)])
                    P.dma("pool", x0fm[j * 128:(j + 1) * 128, i * 128:(i + 1) * 128], ts_,
                          r=[("sqt", jj % 2)], w=[("x0fm", j, i // 4)], slot=f"x0f{jj}")


    def emit_ln(tag, src_key_f, out_write_f, tt):
        P.op("dve", lambda e: e.tensor_scalar(lrow[:, :], prow[:, :], 1.0 / D, None, ALU.mult),
             r=[PS(prow)], w=[("lrow",)])
        P.op("dve", lambda e: e.tensor_tensor(rrow[:, :], lrow[:, :], lrow[:, :], op=ALU.mult),
             r=[("lrow",)], w=[("rrow",)])
        P.op("dve", lambda e: e.scalar_tensor_tensor(
            rrow[:, :], prow2[:, :], 1.0 / D, rrow[:, :], ALU.mult, ALU.subtract),
            r=[PS(prow2), ("rrow",)], w=[("rrow",)])
        P.op("act", lambda e: e.activation(out=rrow[:, :], in_=rrow[:, :], func=AF.Sqrt,
                                           bias=eps_c),
             r=[("rrow",)], w=[("rrow",)])
        P.op("dve", lambda e: e.reciprocal(rrow[:, :], rrow[:, :]),
             r=[("rrow",)], w=[("rrow",)])
        P.op("pe", lambda e: e.matmul(pmb[:, :], oner_f[:, :], rrow[:, :],
                                      start=True, stop=True),
             r=[("rrow",), ("onerf",)], w=[PS(pmb)])
        P.op("pe", lambda e: e.matmul(pbig[3][:, :], oner_f[:, :], lrow[:, :],
                                      start=True, stop=True),
             r=[("lrow",), ("onerf",)], w=[PS(pbig[3])])
        for oc in range(16):
            sq = sqt[oc % 2]
            P.op("dve", lambda e, oc=oc, sq=sq: e.tensor_sub(
                sq[:, :], x1t(oc), pbig[3][:, :]),
                r=[("scrf", oc), PS(pbig[3])], w=[("sqt", oc % 2)])
            out_write_f(oc, sq)

    def wo_wload(og2):
        h = og2 % 2
        co = h * 256
        for j in range(8):
            ws = wstg[j % 2]
            P.dma("sp", ws[:, 0:512],
                  wos[:, og2 * 4096 + j * 512:og2 * 4096 + (j + 1) * 512],
                  r=[("wstg", j % 2)], w=[("wstg", j % 2)], slot=f"ws{j % 2}")
            for k in range(2):
                fc = j * 2 + k
                P.op("act", lambda e, ws=ws, fc=fc, co=co, k=k: e.activation(
                    out=wgrp[fc][:, co:co + 256], in_=ws[:, k * 256:(k + 1) * 256],
                    func=AF.Copy),
                    r=[("wstg", j % 2)], w=[("wgrp", fc, h)])

    wo_wload(0)
    for og2 in range(8):
        h = og2 % 2
        co = h * 256
        if og2 + 1 < 8:
            wo_wload(og2 + 1)
        for oc2 in range(2):
            oc = og2 * 2 + oc2
            for tt in range(4):
                pw = pbig[(oc2 * 4 + tt) % 2]
                for fc in range(16):
                    P.op("pe", lambda e, pw=pw, fc=fc, co=co, oc2=oc2, tt=tt: e.matmul(
                        pw[:, :], wgrp[fc][:, co + oc2 * 128:co + (oc2 + 1) * 128],
                        qar[fc][:, tt * 512:(tt + 1) * 512],
                        start=(fc == 0), stop=(fc == 15)),
                        r=[("wgrp", fc, h), ("qar", fc, tt)], w=[PS(pw)])
                xs_ = xstg[(oc2 * 4 + tt) % 2]
                P.dma("sp", xs_[:, 0:512],
                      x0fm[oc * 128:(oc + 1) * 128, tt * 512:(tt + 1) * 512],
                      r=[("x0fm", oc, 0), ("x0fm", oc, 1),
                         ("x0fm", oc, 2), ("x0fm", oc, 3), ("xstg", (oc2 * 4 + tt) % 2)],
                      w=[("xstg", (oc2 * 4 + tt) % 2)] +
                        ([("eyef",)] if (oc2 * 4 + tt) % 2 == 1 else []),
                      slot=f"xst{(oc2 * 4 + tt) % 2}")
                x1v = dtmp((oc2 * 4 + tt) % 4)
                P.op("dve", lambda e, pw=pw, xs_=xs_, oc=oc, x1v=x1v: e.scalar_tensor_tensor(
                    x1v, pw[:, :], bo_c[:, oc:oc + 1], xs_[:, 0:512],
                    ALU.add, ALU.add),
                    r=[PS(pw), ("boc",), ("xstg", (oc2 * 4 + tt) % 2),
                       ("scrf", 8 + (oc2 * 4 + tt) % 4)],
                    w=[("scrf", 8 + (oc2 * 4 + tt) % 4)])
                P.dma("pool", x2_o[oc * 128:(oc + 1) * 128, tt * 512:(tt + 1) * 512],
                      x1v, r=[("scrf", 8 + (oc2 * 4 + tt) % 4), ("x2w", oc, tt)],
                      w=[("x2w", oc, tt)], slot=f"x2i{(oc * 4 + tt) % 8}")

    for tt in range(4):
        for oc in range(16):
            P.dma("sp", x1t(oc), x2_o[oc * 128:(oc + 1) * 128, tt * 512:(tt + 1) * 512],
                  r=[("x2w", oc, tt), ("scrf", oc)],
                  w=[("scrf", oc)], slot=f"x2r{oc % 8}")
            sq = sqt[oc % 2]
            P.op("act", lambda e, oc=oc, sq=sq: e.activation(
                out=sq[:, :], in_=x1t(oc), func=AF.Square),
                r=[("scrf", oc)], w=[("sqt", oc % 2)])
            P.op("pe", lambda e, oc=oc: e.matmul(
                prow[:, :], onec_f[:, :], x1t(oc), start=(oc == 0), stop=(oc == 15)),
                r=[("scrf", oc), ("onecf",)], w=[PS(prow)])
            P.op("pe", lambda e, sq=sq, oc=oc: e.matmul(
                prow2[:, :], onec_f[:, :], sq[:, :], start=(oc == 0), stop=(oc == 15)),
                r=[("sqt", oc % 2), ("onecf",)], w=[PS(prow2)])

        def wr_x1n(oc, sq, tt=tt):
            P.op("dve", lambda e, oc=oc, sq=sq, tt=tt: e.tensor_mul(
                x1n(oc)[:, tt * 512:(tt + 1) * 512], sq[:, :], pmb[:, :]),
                r=[("sqt", oc % 2), PS(pmb)], w=[("x1n", oc, tt)])
        emit_ln("ln1", None, wr_x1n, tt)

    def ugt(j):
        return SCRB[:, j * 512:(j + 1) * 512]

    def ffn_wload(isup, ig):
        h = ig % 2
        co = h * 256
        G = isup * 8 + ig
        for half in range(2):
            wi = wstgi[half]
            o0 = G * 4096 + half * 2048
            P.dma("sp", wi[:, 0:2048], wgs[:, o0:o0 + 2048],
                  r=[("wstg", half)], w=[("wstg", half)], slot=f"ws{half}")
            for k in range(8):
                hc = half * 8 + k
                P.op("act", lambda e, wi=wi, hc=hc, co=co, k=k: e.activation(
                    out=wgrp[hc][:, co:co + 256], in_=wi[:, k * 256:(k + 1) * 256],
                    func=AF.Copy, bias=-128.0),
                    r=[("wstg", half)], w=[("wgrp", hc, h)])
        for half in range(2):
            wi = wstgi[half]
            o0 = G * 4096 + half * 2048
            P.dma("sp", wi[:, 0:2048], wus[:, o0:o0 + 2048],
                  r=[("wstg", half)] + [("wgrp", half * 8 + k, h) for k in range(8)],
                  w=[("wstg", half)], slot=f"ws{half}")
            for k in range(8):
                hc = half * 8 + k
                trans = ([("scrf", hc // 2)] if isup == 0 and ig < 2 else [])
                P.op("act", lambda e, wi=wi, hc=hc, co=co, k=k: e.activation(
                    out=ugt(hc)[:, co:co + 256], in_=wi[:, k * 256:(k + 1) * 256],
                    func=AF.Copy, bias=-128.0),
                    r=[("wstg", half)] + trans, w=[("ug", hc, h)] + trans)

    ffn_wload(0, 0)
    for isup in range(4):
        for ig in range(8):
            h = ig % 2
            co = h * 256
            if ig + 1 < 8:
                ffn_wload(isup, ig + 1)
            for ic2 in range(2):
                ic = isup * 16 + ig * 2 + ic2
                for tt in range(4):
                    pg = pbig[tt % 2]
                    pu = pbig[2 + tt % 2]
                    for hc in range(16):
                        P.op("pe", lambda e, pg=pg, hc=hc, co=co, ic2=ic2, tt=tt: e.matmul(
                            pg[:, :], wgrp[hc][:, co + ic2 * 128:co + (ic2 + 1) * 128],
                            x1n(hc)[:, tt * 512:(tt + 1) * 512],
                            start=(hc == 0), stop=(hc == 15)),
                            r=[("wgrp", hc, h), ("x1n", hc, tt)], w=[PS(pg)])
                        P.op("pe", lambda e, pu=pu, hc=hc, co=co, ic2=ic2, tt=tt: e.matmul(
                            pu[:, :], ugt(hc)[:, co + ic2 * 128:co + (ic2 + 1) * 128],
                            x1n(hc)[:, tt * 512:(tt + 1) * 512],
                            start=(hc == 0), stop=(hc == 15)),
                            r=[("ug", hc, h), ("x1n", hc, tt)], w=[PS(pu)])
                    gs = gst[tt % 4]
                    us = ust[tt % 4]
                    P.op("dve", lambda e, gs=gs, pg=pg, ic=ic: e.tensor_scalar(
                        gs[:, :], pg[:, :], sg_c[:, ic:ic + 1], None, ALU.mult),
                        r=[PS(pg), ("sgc",)], w=[("pbf", id(gs))])
                    P.op("act", lambda e, gs=gs: e.activation(
                        out=gs[:, :], in_=gs[:, :], func=AF.Silu),
                        r=[("pbf", id(gs))], w=[("pbf", id(gs))])
                    P.op("dve", lambda e, us=us, pu=pu, ic=ic: e.tensor_scalar(
                        us[:, :], pu[:, :], su_c[:, ic:ic + 1], None, ALU.mult),
                        r=[PS(pu), ("suc",)], w=[("obf", id(us))])
                    hloc = ic % 16
                    P.op("dve", lambda e, gs=gs, us=us, hloc=hloc, tt=tt: e.tensor_mul(
                        qar[hloc][:, tt * 512:(tt + 1) * 512], gs[:, :], us[:, :]),
                        r=[("pbf", id(gs)), ("obf", id(us)), ("qar", hloc),
                           ("qar", hloc, tt)],
                        w=[("qar", hloc)])
        def down_wload(og2):
            use_u = og2 % 2 == 1
            dst = ugt if use_u else (lambda j: wgrp[j][:, :])
            dk0 = (lambda j: ("ug", j, 0)) if use_u else (lambda j: ("wgrp", j, 0))
            dk1 = (lambda j: ("ug", j, 1)) if use_u else (lambda j: ("wgrp", j, 1))
            Gd = isup * 8 + og2
            for half in range(2):
                wi = wstgi[half]
                o0 = Gd * 4096 + half * 2048
                P.dma("sp", wi[:, 0:2048], wds[:, o0:o0 + 2048],
                      r=[("wstg", half)], w=[("wstg", half)], slot=f"ws{half}")
                for k in range(8):
                    icd = half * 8 + k
                    P.op("act", lambda e, wi=wi, icd=icd, k=k, dst=dst: e.activation(
                        out=dst(icd)[:, 0:256], in_=wi[:, k * 256:(k + 1) * 256],
                        func=AF.Copy, bias=-128.0),
                        r=[("wstg", half)], w=[dk0(icd), dk1(icd)])

        down_wload(0)
        for og2 in range(8):
            use_u = og2 % 2 == 1
            dst = ugt if use_u else (lambda j: wgrp[j][:, :])
            dkey = (lambda j: ("ug", j, 0)) if use_u else (lambda j: ("wgrp", j, 0))
            if og2 + 1 < 8:
                down_wload(og2 + 1)
            for oc2 in range(2):
                oc = og2 * 2 + oc2
                for tt in range(4):
                    pd = pbig[(oc2 * 4 + tt) % 2]
                    for icd in range(16):
                        P.op("pe", lambda e, pd=pd, icd=icd, oc2=oc2, tt=tt, dst=dst: e.matmul(
                            pd[:, :], dst(icd)[:, oc2 * 128:(oc2 + 1) * 128],
                            qar[icd][:, tt * 512:(tt + 1) * 512],
                            start=(icd == 0), stop=(icd == 15)),
                            r=[dkey(icd), ("qar", icd)], w=[PS(pd)])
                    dj = (oc2 * 4 + tt) % 4
                    P.op("dve", lambda e, pd=pd, dj=dj, oc=oc: e.tensor_scalar(
                        dtmp(dj), pd[:, :], sd_c[:, oc:oc + 1], None, ALU.mult),
                        r=[PS(pd), ("sdc",)], w=[("scrf", 8 + dj)])
                    P.dma("pool", x2_o[oc * 128:(oc + 1) * 128, tt * 512:(tt + 1) * 512],
                          dtmp(dj), r=[("scrf", 8 + dj), ("x2w", oc, tt), ("x2d", oc, tt)],
                          w=[("x2d", oc, tt)], slot=f"x2a{(oc * 4 + tt) % 8}",
                          accum_op=ALU.add)
        if isup + 1 < 4:
            ffn_wload(isup + 1, 0)

    for tt in range(4):
        for oc in range(16):
            P.dma("sp", x1t(oc), x2_o[oc * 128:(oc + 1) * 128, tt * 512:(tt + 1) * 512],
                  r=[("x2d", oc, tt), ("scrf", oc), ("ug", 2 * oc, 0), ("ug", 2 * oc, 1),
                     ("ug", 2 * oc + 1, 0), ("ug", 2 * oc + 1, 1)],
                  w=[("scrf", oc), ("ug", 2 * oc, 0), ("ug", 2 * oc, 1),
                     ("ug", 2 * oc + 1, 0), ("ug", 2 * oc + 1, 1)], slot=f"x2r{oc}")
            sq = sqt[oc % 2]
            P.op("act", lambda e, oc=oc, sq=sq: e.activation(
                out=sq[:, :], in_=x1t(oc), func=AF.Square),
                r=[("scrf", oc)], w=[("sqt", oc % 2)])
            P.op("pe", lambda e, oc=oc: e.matmul(
                prow[:, :], onec_f[:, :], x1t(oc), start=(oc == 0), stop=(oc == 15)),
                r=[("scrf", oc), ("onecf",)], w=[PS(prow)])
            P.op("pe", lambda e, sq=sq, oc=oc: e.matmul(
                prow2[:, :], onec_f[:, :], sq[:, :], start=(oc == 0), stop=(oc == 15)),
                r=[("sqt", oc % 2), ("onecf",)], w=[PS(prow2)])

        def wr_x2n(oc, sq, tt=tt):
            ob = obf[oc % 4]
            P.op("dve", lambda e, sq=sq, ob=ob: e.tensor_mul(ob[:, :], sq[:, :], pmb[:, :]),
                 r=[("sqt", oc % 2), PS(pmb)], w=[("obf", id(ob))])
            P.dma("pool", x2n_o[oc * 128:(oc + 1) * 128, tt * 512:(tt + 1) * 512], ob[:, :],
                  r=[("obf", id(ob))], w=[("x2no", oc, tt)], slot=f"ost{oc % 4}")
        emit_ln("ln2", None, wr_x2n, tt)

    out_keys = [k for k in P.last_w if k and k[0] in ("x2d", "x2no", "x2w")]
    P.op("sp", lambda e: e.nop(), r=out_keys)

    fc = P.setup_consts(nc, ctx)
    sems = P.finalize(nc, ctx)
    ctx.close()
    return nc, fc


def run_L2(st, l2in):
    nc, fc = build_L2()
    in_maps = []
    for c in range(8):
        d = dict(l2in[c])
        d.update({"x0c": st["x0c"][c], "wos": st["wos"], "bo": st["bo"],
                  "masks": st["masks"], "wgs": st["wgs"], "wus": st["wus"],
                  "wds": st["wds"], "sg": st["sg"], "su": st["su"], "sd": st["sd"],
                  "eye": st["eye"], "fconsts": fc})
        in_maps.append(d)
    res = run_bass_kernel_spmd(nc, in_maps, core_ids=list(range(8)))
    return res.results



def build_L3(micro=99):
    full = micro >= 99
    """GLA projections, decay cumsums, intra-chunk attention + local states."""
    nc = bass.Bass("TRN2", target_bir_lowering=False)
    x2n_d = nc.dram_tensor("x2n", [D, TPC], BF16, kind="ExternalInput").ap()
    wqb = nc.dram_tensor("wqb", [D, GKD], F32, kind="ExternalInput").ap()
    wkb = nc.dram_tensor("wkb", [D, GKD], F32, kind="ExternalInput").ap()
    wvb = nc.dram_tensor("wvb", [D, D], F32, kind="ExternalInput").ap()
    wgk1 = nc.dram_tensor("wgk1", [D, LR], F32, kind="ExternalInput").ap()
    wgk2 = nc.dram_tensor("wgk2", [LR, GKD], F32, kind="ExternalInput").ap()
    bgk2 = nc.dram_tensor("bgk2", [1, GKD], F32, kind="ExternalInput").ap()
    tri_d = nc.dram_tensor("tri", [128, 384], F32, kind="ExternalInput").ap()
    smask_d = nc.dram_tensor("smask", [CC, CC], F32, kind="ExternalInput").ap()
    eye_d = nc.dram_tensor("eye", [128, 128], F32, kind="ExternalInput").ap()

    oi_o = nc.dram_tensor("oi", [TPC, D], F32, kind="ExternalOutput").ap()
    qg_o = nc.dram_tensor("qg", [GKD, TPC], BF16, kind="ExternalOutput").ap()
    acl_o = nc.dram_tensor("acl", [256, 1024], F32, kind="ExternalOutput").ap()
    gcl_o = nc.dram_tensor("gcl", [256, 8], F32, kind="ExternalOutput").ap()

    ctx = ExitStack()
    sb = lambda *a: ctx.enter_context(nc.sbuf_tensor(*a))
    pst = lambda *a: ctx.enter_context(nc.psum_tensor(*a))
    P = Prog()

    xn = [sb(f"xn{j}", [128, TPC], BF16) for j in range(16)]
    qsp = sb("qsp", [128, 4096], BF16)
    ksp = sb("ksp", [128, 4096], BF16)
    vsp = [sb(f"vsp{j}", [128, D], BF16) for j in range(4)]
    zsp = [sb(f"zsp{j}", [128, GKD], F32) for j in range(4)]
    z1 = sb("z1", [LR, 512], BF16)
    wgrp = [sb(f"wgrpL3{j}", [128, 512], BF16) for j in range(16)]
    wstg = [sb(f"wstgL3{j}", [128, 512], F32) for j in range(2)]
    tri_s = sb("tri_s", [128, 384], F32)
    smask2 = sb("smask2", [128, CC], BF16)
    eye_b = sb("eye_b", [128, 128], BF16)
    bgk_b = sb("bgk_b", [1, GKD], BF16)
    ones_rb = sb("ones_rb", [1, 128], BF16)
    b_sb = [sb(f"b_sb{j}", [128, 128], F32) for j in range(4)]
    bs_sb = [sb(f"bs_sb{j}", [128, 128], F32) for j in range(4)]
    gm_sb = [sb(f"gm_sb{j}", [128, 128], F32) for j in range(4)]
    eq_sb = [sb(f"eq_sb{j}", [128, 128], F32) for j in range(4)]
    qt_b = [sb(f"qt_b{j}", [128, 128], BF16) for j in range(4)]
    kt_b = [sb(f"kt_b{j}", [128, 128], BF16) for j in range(4)]
    kh_b = [sb(f"kh_b{j}", [128, 128], BF16) for j in range(4)]
    kht = [sb(f"kht{j}", [128, 128], BF16) for j in range(4)]
    qg_b = [sb(f"qg_b{j}", [128, 128], BF16) for j in range(4)]
    qc_b = [sb(f"qc_b{j}", [128, 128], BF16) for j in range(4)]
    eoffc = [sb(f"eoffc{j}", [128, 1], F32) for j in range(4)]
    dt_s = [sb(f"dt_s{j}", [128, 8], F32) for j in range(4)]
    eg_s = [sb(f"eg_s{j}", [128, 2], F32) for j in range(4)]
    off_s = sb("off_s", [128, 16], F32)
    ktrun = [sb(f"ktrun{j}", [128, CC], BF16) for j in range(4)]
    pcs = [sb(f"pcs{j}", [128, 128], BF16) for j in range(4)]
    S_all = sb("S_all", [128, 1024], F32)
    S_bf = sb("S_bf", [128, 1024], BF16)
    ost = [sb(f"ostL3{j}", [128, D], F32) for j in range(2)]
    obf2 = [sb(f"obf2_{j}", [128, 512], BF16) for j in range(2)]

    pq = [pst(f"pqL3{j}", [128, 512], F32) for j in range(2)]
    pcum = [pst(f"pcumL3{j}", [128, 128], F32) for j in range(3)]
    psc0 = pst("pscL3", [128, 128], F32)
    ptb = pst("ptbL3", [128, 128], BF16)
    pov0 = pst("povL30", [128, 128], F32)
    psc = [psc0[:, 0:16], psc0[:, 32:48], psc0[:, 64:80], psc0[:, 96:112]]
    pov = [pov0[0:64, 0:128], pov0[64:128, 0:128],
           pq[1][0:64, 0:128], pq[1][64:128, 0:128]]

    def PSL(t):
        ex = [PS(pov[2]), PS(pov[3])] if t is pq[1] else []
        return [PS(t)] + ex

    def PS(t):
        return ("PS", id(t))

    P.dma("sp", tri_s[:, :], tri_d[:, :], w=[("tri",)], slot="c1")
    P.dma("sp", wstg[0][0:64, 0:64], smask_d[:, :], w=[("wstg", 0)], slot="ws0")
    P.op("act", lambda e: e.activation(out=smask2[0:64, :], in_=wstg[0][0:64, 0:64],
                                       func=AF.Copy),
         r=[("wstg", 0)], w=[("smask",)])
    P.op("act", lambda e: e.activation(out=smask2[64:128, :], in_=wstg[0][0:64, 0:64],
                                       func=AF.Copy),
         r=[("wstg", 0)], w=[("smask",)])
    P.dma("sp", wstg[1][:, 0:128], eye_d[:, :], w=[("wstg", 1)], slot="ws1")
    P.op("act", lambda e: e.activation(out=eye_b[:, :], in_=wstg[1][:, 0:128], func=AF.Copy),
         r=[("wstg", 1)], w=[("eyeb",)])
    P.dma("sp", wstg[1][0:1, 0:512], bgk2[0:1, 0:512],
          r=[("wstg", 1), ("eyeb",)], w=[("wstg", 1)], slot="ws1")
    P.op("act", lambda e: e.activation(out=bgk_b[0:1, 0:512], in_=wstg[1][0:1, 0:512],
                                       func=AF.Copy),
         r=[("wstg", 1)], w=[("bgkb",)])
    P.dma("sp", wstg[1][0:1, 0:512], bgk2[0:1, 512:1024],
          r=[("wstg", 1), ("bgkb",)], w=[("wstg", 1)], slot="ws1")
    P.op("act", lambda e: e.activation(out=bgk_b[0:1, 512:1024], in_=wstg[1][0:1, 0:512],
                                       func=AF.Copy),
         r=[("wstg", 1)], w=[("bgkb",)])
    P.op("dve", lambda e: e.memset(ones_rb[:, :], 1.0), w=[("onesrb",)])
    P.op("dve", lambda e: e.memset(S_all[:, :], 0.0),
         w=[("sall", j) for j in range(8)])
    P.op("dve", lambda e: e.memset(S_bf[:, :], 0.0),
         w=[("sbf", j) for j in range(8)])
    P.op("dve", lambda e: e.memset(psc0[:, :], 0.0),
         w=[PS(p) for p in psc])
    P.op("dve", lambda e: e.memset(off_s[:, :], 0.0),
         w=[("offs", j) for j in range(16)])

    for j in range(16):
        P.dma("sp", xn[j][:, :], x2n_d[j * 128:(j + 1) * 128, :], w=[("xn", j)],
              slot="xnload")

    def stream_w(dram, cols0):
        for hc in range(16):
            ws = wstg[hc % 2]
            P.dma("sp", ws[:, :], dram[hc * 128:(hc + 1) * 128, cols0:cols0 + 512],
                  r=[("wstg", hc % 2)], w=[("wstg", hc % 2)], slot=f"ws{hc % 2}")
            P.op("act", lambda e, ws=ws, hc=hc: e.activation(
                out=wgrp[hc][:, :], in_=ws[:, :], func=AF.Copy),
                r=[("wstg", hc % 2)], w=[("wgrp", hc)])

    for ss in range(4 if (micro >= 1 or full) else 0):
        t0 = ss * 512
        for which, dram, dst, scale in (((0, wqb, qsp, 0.125), (1, wkb, ksp, 1.0)) if (full or micro not in (14, 15, 16)) else ()):
            for cg in range(2):
                stream_w(dram, cg * 512)
                for kc4 in range(4):
                    kdc = cg * 4 + kc4
                    pb = pq[kc4 % 2]
                    for hc in range(16):
                        P.op("pe", lambda e, pb=pb, hc=hc, kc4=kc4, t0=t0: e.matmul(
                            pb[:, :], wgrp[hc][:, kc4 * 128:(kc4 + 1) * 128],
                            xn[hc][:, t0:t0 + 512],
                            start=(hc == 0), stop=(hc == 15)),
                            r=[("wgrp", hc), ("xn", hc)], w=PSL(pb))
                    P.op("dve", lambda e, pb=pb, dst=dst, kdc=kdc, scale=scale: e.tensor_scalar(
                        dst[:, kdc * 512:(kdc + 1) * 512], pb[:, :], scale, None, ALU.mult),
                        r=PSL(pb), w=[("qk", id(dst), kdc)])
        for dg in range(4 if (full or ((micro >= 12 or micro < 10) and micro not in (14, 15, 16))) else 0):
            stream_w(wvb, dg * 512)
            for tloc in range(4):
                pb = pq[tloc % 2]
                for hc in range(16):
                    P.op("pe", lambda e, pb=pb, hc=hc, tloc=tloc, t0=t0: e.matmul(
                        pb[:, :], xn[hc][:, t0 + tloc * 128:t0 + (tloc + 1) * 128],
                        wgrp[hc][:, :], start=(hc == 0), stop=(hc == 15)),
                        r=[("wgrp", hc), ("xn", hc)], w=PSL(pb))
                P.op("act", lambda e, pb=pb, tloc=tloc, dg=dg: e.activation(
                    out=vsp[tloc][:, dg * 512:(dg + 1) * 512], in_=pb[:, :], func=AF.Copy),
                    r=PSL(pb), w=[("vsp", tloc, dg)])
        for hc in range(16 if (full or micro >= 13 or micro < 10) else 0):
            ws = wstg[hc % 2]
            P.dma("sp", ws[:, 0:LR], wgk1[hc * 128:(hc + 1) * 128, :],
                  r=[("wstg", hc % 2)], w=[("wstg", hc % 2)], slot=f"ws{hc % 2}")
            P.op("act", lambda e, ws=ws, hc=hc: e.activation(
                out=wgrp[hc][:, 0:LR], in_=ws[:, 0:LR], func=AF.Copy),
                r=[("wstg", hc % 2)], w=[("wgrp", hc)])
        pb = pq[0]
        for hc in range(16 if (full or micro >= 13 or micro < 10) else 0):
            P.op("pe", lambda e, pb=pb, hc=hc, t0=t0: e.matmul(
                pb[0:LR, 0:512], wgrp[hc][:, 0:LR], xn[hc][:, t0:t0 + 512],
                start=(hc == 0), stop=(hc == 15)),
                r=[("wgrp", hc), ("xn", hc)], w=PSL(pb))
        if full or micro >= 13 or micro < 10:
            P.op("act", lambda e, pb=pb: e.activation(out=z1[:, :], in_=pb[0:LR, 0:512],
                                                      func=AF.Copy),
                 r=PSL(pb), w=[("z1",)])
        for hc2 in range(2 if (full or micro >= 13 or micro < 10) else 0):
            ws = wstg[hc2]
            P.dma("sp", ws[0:LR, :], wgk2[:, hc2 * 512:(hc2 + 1) * 512],
                  r=[("wstg", hc2)], w=[("wstg", hc2)], slot=f"ws{hc2}")
            P.op("act", lambda e, ws=ws, hc2=hc2: e.activation(
                out=wgrp[hc2][0:LR, :], in_=ws[0:LR, :], func=AF.Copy),
                r=[("wstg", hc2)], w=[("wgrp", hc2)])
        for tloc in range(4 if (full or micro >= 15 or micro < 10 or micro == 13) else 0):
            for hc2 in range(2):
                pb = pq[hc2]
                P.op("pe", lambda e, pb=pb, tloc=tloc, hc2=hc2: e.matmul(
                    pb[:, :], z1[:, tloc * 128:(tloc + 1) * 128], wgrp[hc2][0:LR, :],
                    start=True, stop=False),
                    r=[("z1",), ("wgrp", hc2)], w=PSL(pb))
                P.op("pe", lambda e, pb=pb, hc2=hc2: e.matmul(
                    pb[:, :], ones_rb[:, :], bgk_b[0:1, hc2 * 512:(hc2 + 1) * 512],
                    start=False, stop=True),
                    r=[("onesrb",), ("bgkb",)], w=PSL(pb))
                if micro == 15 and not full:
                    continue
                nops = micro - 130 if (130 < micro < 136 and not full) else 5
                zt = zsp[tloc]
                c0, c1 = hc2 * 512, (hc2 + 1) * 512
                P.op("dve", lambda e, pb=pb, zt=zt, c0=c0, c1=c1: e.tensor_scalar_min(
                    zt[:, c0:c1], pb[:, :], 0.0),
                    r=PSL(pb), w=[("zsp", tloc, hc2)])
                ab = obf2[hc2]
                if nops >= 2:
                    P.op("dve", lambda e, pb=pb, zt=zt, ab=ab, c0=c0, c1=c1: e.scalar_tensor_tensor(
                        ab[:, :], zt[:, c0:c1], 2.0, pb[:, :], ALU.mult, ALU.subtract),
                        r=[("zsp", tloc, hc2)] + PSL(pb), w=[("ab", id(ab))])
                if nops >= 3:
                    P.op("act", lambda e, ab=ab: e.activation(
                        out=ab[:, :], in_=ab[:, :], func=AF.Exp),
                        r=[("ab", id(ab))], w=[("ab", id(ab))])
                if nops >= 4:
                    P.op("act", lambda e, ab=ab: e.activation(
                        out=ab[:, :], in_=ab[:, :], func=AF.Ln, bias=1.0),
                        r=[("ab", id(ab))], w=[("ab", id(ab))])
                if nops >= 5:
                    P.op("dve", lambda e, zt=zt, ab=ab, c0=c0, c1=c1: e.tensor_sub(
                        zt[:, c0:c1], zt[:, c0:c1], ab[:, :]),
                        r=[("zsp", tloc, hc2), ("ab", id(ab))], w=[("zsp", tloc, hc2)])

        for tloc in range(4 if (full or (micro >= 2 and micro < 10)) else 0):
            sp = ss * 4 + tloc
            t0l = t0 + tloc * 128
            osb = ost[tloc % 2]
            P.op("dve", lambda e, osb=osb: e.memset(osb[:, :], 0.0),
                 r=[("ostW", tloc % 2)],
                 w=[("ost", tloc % 2, h) for h in range(16)])

            def emit_prep(kdc, tloc=tloc, sp=sp, t0l=t0l):
                gk = zsp[tloc]
                for w3, bank, dstt in ((0, pcum[0], b_sb), (1, pcum[1], bs_sb),
                                       (2, pcum[2], gm_sb)):
                    P.op("pe", lambda e, gk=gk, kdc=kdc, w3=w3, bank=bank: e.matmul(
                        bank[:, :], gk[:, kdc * 128:(kdc + 1) * 128],
                        tri_s[:, w3 * 128:(w3 + 1) * 128], start=True, stop=True),
                        r=[("zsp", tloc, kdc // 4), ("tri",)], w=[PS(bank)])
                    P.op("dve", lambda e, bank=bank, dstt=dstt, kdc=kdc: e.tensor_copy(
                        dstt[kdc % 4][:, :], bank[:, :]),
                        r=[PS(bank)], w=[("cums", id(dstt[kdc % 4]))])
                bb, bs, gm = b_sb[kdc % 4], bs_sb[kdc % 4], gm_sb[kdc % 4]
                qt, kt, kh = qt_b[kdc % 4], kt_b[kdc % 4], kh_b[kdc % 4]
                khtt = kht[kdc % 4]
                eqs = eq_sb[kdc % 4]
                rot = kdc % 4
                P.op("dve", lambda e, bb=bb, bs=bs, eqs=eqs: e.tensor_sub(
                    eqs[:, :], bb[:, :], bs[:, :]),
                    r=[("cums", id(bb)), ("cums", id(bs))], w=[("eq", rot)])
                P.op("act", lambda e, eqs=eqs: e.activation(
                    out=eqs[:, :], in_=eqs[:, :], func=AF.Exp),
                    r=[("eq", rot)], w=[("eq", rot)])
                P.op("dve", lambda e, qt=qt, eqs=eqs, kdc=kdc, tloc=tloc: e.tensor_mul(
                    qt[:, :], qsp[:, kdc * 512 + tloc * 128:kdc * 512 + (tloc + 1) * 128],
                    eqs[:, :]),
                    r=[("qk", id(qsp), kdc), ("eq", rot)], w=[("qt", id(qt))])
                P.op("dve", lambda e, eqs=eqs: e.reciprocal(eqs[:, :], eqs[:, :]),
                     r=[("eq", rot)], w=[("eq", rot)])
                P.op("dve", lambda e, kt=kt, eqs=eqs, kdc=kdc, tloc=tloc: e.tensor_mul(
                    kt[:, :], ksp[:, kdc * 512 + tloc * 128:kdc * 512 + (tloc + 1) * 128],
                    eqs[:, :]),
                    r=[("qk", id(ksp), kdc), ("eq", rot)], w=[("kt", id(kt))])
                P.op("act", lambda e, gm=gm: e.activation(out=gm[:, :], in_=gm[:, :],
                                                          func=AF.Exp),
                     r=[("cums", id(gm))], w=[("cums", id(gm))])
                P.op("dve", lambda e, kh=kh, gm=gm, kdc=kdc, tloc=tloc: e.tensor_mul(
                    kh[:, :], ksp[:, kdc * 512 + tloc * 128:kdc * 512 + (tloc + 1) * 128],
                    gm[:, :]),
                    r=[("qk", id(ksp), kdc), ("cums", id(gm))], w=[("kh", id(kh))])
                P.op("pe", lambda e, kh=kh: e.transpose(ptb[:, :], kh[:, :], eye_b[:, :]),
                     r=[("kh", id(kh)), ("eyeb",)], w=[PS(ptb)])
                P.op("dve", lambda e, khtt=khtt: e.tensor_copy(khtt[:, :], ptb[:, :]),
                     r=[PS(ptb)], w=[("kht", id(khtt))])
                qcb, qgb, eof = qc_b[kdc % 4], qg_b[kdc % 4], eoffc[kdc % 4]
                P.op("act", lambda e, bb=bb, eqs=eqs: e.activation(
                    out=eqs[:, :], in_=bb[:, :], func=AF.Exp),
                    r=[("cums", id(bb))], w=[("eq", rot)])
                P.op("dve", lambda e, qcb=qcb, eqs=eqs, kdc=kdc, tloc=tloc: e.tensor_mul(
                    qcb[:, :], qsp[:, kdc * 512 + tloc * 128:kdc * 512 + (tloc + 1) * 128],
                    eqs[:, :]),
                    r=[("qk", id(qsp), kdc), ("eq", rot)], w=[("qcb", rot)])
                P.op("act", lambda e, eof=eof, kdc=kdc: e.activation(
                    out=eof[:, :], in_=off_s[:, kdc:kdc + 1], func=AF.Exp),
                    r=[("offs", kdc)], w=[("eoffc", rot)])
                P.op("dve", lambda e, qgb=qgb, qcb=qcb, eof=eof: e.tensor_scalar(
                    qgb[:, :], qcb[:, :], eof[:, 0:1], None, ALU.mult),
                    r=[("qcb", rot), ("eoffc", rot)], w=[("qgb", rot)])
                P.dma("pool", qg_o[kdc * 128:(kdc + 1) * 128, t0l:t0l + 128], qgb[:, :],
                      r=[("qgb", rot)], w=[("qgout", kdc, sp)], slot="qgst")
                dts, egs = dt_s[kdc % 4], eg_s[kdc % 4]
                for cc2 in range(2):
                    base = cc2 * 64
                    for u in range(1, 4):
                        idx = cc2 * 3 + (u - 1)
                        P.op("dve", lambda e, bs=bs, dts=dts, base=base, u=u,
                             idx=idx: e.tensor_sub(
                            dts[:, idx:idx + 1],
                            bs[:, base + 16 * u:base + 16 * u + 1],
                            bs[:, base + 16 * (u - 1):base + 16 * (u - 1) + 1]),
                            r=[("cums", id(bs)), ("dts", rot)], w=[("dts", rot)])
                    P.op("act", lambda e, bb=bb, egs=egs, base=base, cc2=cc2: e.activation(
                        out=egs[:, cc2:cc2 + 1], in_=bb[:, base + 63:base + 64],
                        func=AF.Exp),
                        r=[("cums", id(bb)), ("egs", rot, cc2)], w=[("egs", rot, cc2)])
                P.op("act", lambda e, dts=dts: e.activation(
                    out=dts[:, 0:6], in_=dts[:, 0:6], func=AF.Exp),
                    r=[("dts", rot)], w=[("dts", rot)])

            def chunk_gen(kdc, tloc=tloc, sp=sp, t0l=t0l, osb=osb):
                rot = kdc % 4
                bb, bs = b_sb[rot], bs_sb[rot]
                qt, kt = qt_b[rot], kt_b[rot]
                khtt = kht[rot]
                qcb = qc_b[rot]
                ktr = ktrun[rot]
                pc = pcs[rot]
                psb = psc[rot]
                pob = pov[rot]
                dts = dt_s[rot]
                egs = eg_s[rot]
                for cc2 in range(2 if (full or micro >= 4) else 0):
                    base = cc2 * 64
                    P.op("dve", lambda e, kt=kt, ktr=ktr, base=base: e.tensor_copy(
                        ktr[:, :], kt[:, base:base + 64]),
                        r=[("kt", id(kt))], w=[("ktrun", rot)])
                    yield
                    for u in range(4):
                        if u > 0:
                            idx = cc2 * 3 + (u - 1)
                            P.op("dve", lambda e, ktr=ktr, dts=dts, u=u, idx=idx: e.tensor_scalar(
                                ktr[:, 0:16 * u], ktr[:, 0:16 * u], dts[:, idx:idx + 1],
                                None, ALU.mult),
                                r=[("ktrun", rot), ("dts", rot)], w=[("ktrun", rot)])
                        for h2 in range(2):
                            hb = 64 * h2
                            nk = 16 * (u + 1)
                            P.op("pe", lambda e, hb=hb, nk=nk, u=u, qt=qt, ktr=ktr,
                                 psb=psb, base=base: e.matmul(
                                psb[base:base + nk, 0:16],
                                ktr[hb:hb + 64, 0:nk],
                                qt[hb:hb + 64, base + u * 16:base + u * 16 + 16],
                                start=True, stop=True),
                                r=[("ktrun", rot), ("qt", id(qt))], w=[PS(psb)])
                            P.op("dve", lambda e, hb=hb, u=u, pc=pc, psb=psb,
                                 base=base: e.tensor_mul(
                                pc[base:base + 64, hb + u * 16:hb + u * 16 + 16],
                                psb[base:base + 64, 0:16],
                                smask2[base:base + 64, u * 16:u * 16 + 16]),
                                r=[PS(psb), ("smask",)], w=[("pcs", rot)])
                        yield
                    for h2 in range(2):
                        hb = 64 * h2
                        head = kdc * 2 + h2
                        P.op("pe", lambda e, hb=hb, pob=pob, pc=pc, cc2=cc2, tloc=tloc,
                             head=head, base=base: e.matmul(
                            pob[:, :], pc[base:base + 64, hb:hb + 64],
                            vsp[tloc][base:base + 64, head * 128:(head + 1) * 128],
                            start=True, stop=True),
                            r=[("pcs", rot), ("vsp", tloc, head // 4)], w=[PS(pob)])
                        P.op("dve", lambda e, pob=pob, osb=osb, base=base,
                             head=head: e.tensor_add(
                            osb[base:base + 64, head * 128:(head + 1) * 128],
                            osb[base:base + 64, head * 128:(head + 1) * 128],
                            pob[:, :]),
                            r=[PS(pob), ("ost", tloc % 2, head)],
                            w=[("ost", tloc % 2, head)])
                        P.op("pe", lambda e, hb=hb, pob=pob, base=base, qcb=qcb,
                             kdc=kdc: e.matmul(
                            pob[:, :], qcb[hb:hb + 64, base:base + 64],
                            S_bf[hb:hb + 64, kdc * 128:(kdc + 1) * 128],
                            start=True, stop=True),
                            r=[("qcb", rot), ("sbf", kdc)], w=[PS(pob)])
                        P.op("dve", lambda e, pob=pob, osb=osb, base=base,
                             head=head: e.tensor_add(
                            osb[base:base + 64, head * 128:(head + 1) * 128],
                            osb[base:base + 64, head * 128:(head + 1) * 128],
                            pob[:, :]),
                            r=[PS(pob), ("ost", tloc % 2, head)],
                            w=[("ost", tloc % 2, head)])
                        yield
                    for h2 in range(2):
                        hb = 64 * h2
                        head = kdc * 2 + h2
                        P.op("pe", lambda e, khtt=khtt, pob=pob, hb=hb, base=base,
                             tloc=tloc, head=head: e.matmul(
                            pob[:, :], khtt[base:base + 64, hb:hb + 64],
                            vsp[tloc][base:base + 64, head * 128:(head + 1) * 128],
                            start=True, stop=True),
                            r=[("kht", id(khtt)), ("vsp", tloc, head // 4)], w=[PS(pob)])
                        P.op("dve", lambda e, pob=pob, egs=egs, hb=hb, kdc=kdc,
                             cc2=cc2: e.scalar_tensor_tensor(
                            S_all[hb:hb + 64, kdc * 128:(kdc + 1) * 128],
                            S_all[hb:hb + 64, kdc * 128:(kdc + 1) * 128],
                            egs[hb:hb + 64, cc2:cc2 + 1],
                            pob[:, :], ALU.mult, ALU.add),
                            r=[PS(pob), ("egs", rot, cc2), ("sall", kdc)],
                            w=[("sall", kdc)])
                        P.op("act", lambda e, hb=hb, kdc=kdc: e.activation(
                            out=S_bf[hb:hb + 64, kdc * 128:(kdc + 1) * 128],
                            in_=S_all[hb:hb + 64, kdc * 128:(kdc + 1) * 128],
                            func=AF.Copy),
                            r=[("sall", kdc)], w=[("sbf", kdc)])
                        yield
                    P.op("dve", lambda e, bb=bb, kdc=kdc, base=base: e.tensor_add(
                        off_s[:, kdc:kdc + 1], off_s[:, kdc:kdc + 1],
                        bb[:, base + 63:base + 64]),
                        r=[("cums", id(bb)), ("offs", kdc)], w=[("offs", kdc)])
                    P.op("dve", lambda e, bb=bb, kdc=kdc, base=base: e.tensor_add(
                        off_s[:, 8 + kdc:8 + kdc + 1], off_s[:, 8 + kdc:8 + kdc + 1],
                        bb[:, base + 63:base + 64]),
                        r=[("cums", id(bb)), ("offs", 8 + kdc)], w=[("offs", 8 + kdc)])
                    yield

            if full or micro >= 3:
                for p4 in range(2):
                    for j in range(4):
                        emit_prep(4 * p4 + j)
                    alive = [chunk_gen(4 * p4 + j) for j in range(4)]
                    while alive:
                        for g in list(alive):
                            try:
                                next(g)
                            except StopIteration:
                                alive.remove(g)
            P.dma("pool", oi_o[t0l:t0l + 128, :], osb[:, :],
                  r=[("ost", tloc % 2, h) for h in range(16)],
                  w=[("ostW", tloc % 2), ("oiout", sp)],
                  slot=f"oist{tloc % 2}")
            if sp in (7, 15):
                si = 0 if sp == 7 else 1
                P.dma("pool", acl_o[si * 128:(si + 1) * 128, :], S_all[:, :],
                      r=[("sall", j) for j in range(8)], w=[("aclout", si)],
                      slot="stout")
                P.dma("pool", gcl_o[si * 128:(si + 1) * 128, :], off_s[:, 8:16],
                      r=[("offs", 8 + j) for j in range(8)], w=[("gclout", si)],
                      slot="stout")
                if sp == 7:
                    P.op("dve", lambda e: e.memset(S_all[:, :], 0.0),
                         r=[("aclout", 0)] + [("sall", j) for j in range(8)],
                         w=[("sall", j) for j in range(8)])
                    P.op("dve", lambda e: e.memset(S_bf[:, :], 0.0),
                         r=[("sbf", j) for j in range(8)],
                         w=[("sbf", j) for j in range(8)])
                    P.op("dve", lambda e: e.memset(off_s[:, :], 0.0),
                         r=[("gclout", 0), ("aclout", 0)] +
                           [("offs", j) for j in range(16)],
                         w=[("offs", j) for j in range(16)])

    out_keys = [k for k in P.last_w if k and k[0] in ("oiout", "qgout", "aclout", "gclout")]
    P.op("sp", lambda e: e.nop(), r=out_keys)

    fc = P.setup_consts(nc, ctx)
    sems = P.finalize(nc, ctx)
    ctx.close()
    return nc, fc




def build_L4():
    nc = bass.Bass("TRN2", target_bir_lowering=False)
    oi_d = nc.dram_tensor("oi", [TPC, D], F32, kind="ExternalInput").ap()
    qg_d = nc.dram_tensor("qg", [GKD, TPC], BF16, kind="ExternalInput").ap()
    pA_d = nc.dram_tensor("pA", [1792, 1024], F32, kind="ExternalInput").ap()
    pG_d = nc.dram_tensor("pG", [1792, 8], F32, kind="ExternalInput").ap()
    x2n_d = nc.dram_tensor("x2n", [D, TPC], BF16, kind="ExternalInput").ap()
    x2tm_d = nc.dram_tensor("x2tm", [TPC, D], F32, kind="ExternalInput").ap()
    wgb_d = nc.dram_tensor("wgb", [128, 32768], F32, kind="ExternalInput").ap()
    wob_d = nc.dram_tensor("wob", [D, D], F32, kind="ExternalInput").ap()
    nw_d = nc.dram_tensor("nw", [128, 1], F32, kind="ExternalInput").ap()
    wgs = nc.dram_tensor("wgs", [128, 131072], U8, kind="ExternalInput").ap()
    wus = nc.dram_tensor("wus", [128, 131072], U8, kind="ExternalInput").ap()
    wds = nc.dram_tensor("wds", [128, 131072], U8, kind="ExternalInput").ap()
    sg_d = nc.dram_tensor("sg", [128, 64], F32, kind="ExternalInput").ap()
    su_d = nc.dram_tensor("su", [128, 64], F32, kind="ExternalInput").ap()
    sdr_d = nc.dram_tensor("sdr", [1, D], F32, kind="ExternalInput").ap()
    eye_d = nc.dram_tensor("eye", [128, 128], F32, kind="ExternalInput").ap()

    y_o = nc.dram_tensor("y", [TPC, D], F32, kind="ExternalOutput").ap()
    onrm = nc.dram_tensor("onrm", [TPC, D], F32, kind="Internal").ap()

    ctx = ExitStack()
    sb = lambda *a: ctx.enter_context(nc.sbuf_tensor(*a))
    pst = lambda *a: ctx.enter_context(nc.psum_tensor(*a))
    P = Prog()

    xn = [sb(f"xnL4{j}", [128, TPC], BF16) for j in range(16)]
    qar = [sb(f"qarL4{j}", [128, TPC], BF16) for j in range(16)]
    SCRF = sb("scrfL4", [128, 8192], F32)
    SCRB = SCRF[:, :].bitcast(BF16)
    wgrp = [sb(f"wgrpL4{j}", [128, 512], BF16) for j in range(16)]
    wstg = [sb(f"wstgL4{j}", [128, 512], F32) for j in range(2)]
    wstgi = [wstg[j][:, :].bitcast(U8) for j in range(2)]
    spre = [sb(f"spre{j}", [128, 1024], BF16) for j in range(2)]
    oisb0 = sb("oisb0", [128, D], F32)
    qgst0 = sb("qgstL40", [128, 1024], BF16)
    onst = [sb(f"onst{j}", [128, 512], F32) for j in range(2)]
    x2st = [sb(f"x2st{j}", [128, 512], F32) for j in range(2)]
    gsil = [sb(f"gsil{j}", [128, 512], BF16) for j in range(2)]
    ogsp = [sb(f"ogsp{j}", [128, 512], BF16) for j in range(2)]
    x3nt = oisb0[:, :].bitcast(BF16)[:, 0:D]
    sdbt = onst[0]
    eye_b = sb("eye_bL4", [128, 128], BF16)
    nw_c = sb("nw_c", [128, 1], F32)
    sg_c = sb("sg_cL4", [128, 64], F32)
    su_c = sb("su_cL4", [128, 64], F32)
    gw_s = sb("gw_s", [128, 16], F32)
    eg_s = sb("eg_sL4", [128, 16], F32)
    stat = sb("statL4", [128, 40], F32)
    ones_rf = sb("ones_rfL4", [1, 128], F32)

    pbig = [pst(f"pbigL4{j}", [128, 512], F32) for j in range(4)]
    ptb = pst("ptbL4", [128, 128], BF16)

    def PS(t):
        return ("PS", id(t))

    epsg_c = P.c(EPS_GLA)
    epsl_c = P.c(EPS_LN)
    invdv_c = P.c(1.0 / GDV)

    P.dma("sp", nw_c[:, :], nw_d[:, :], w=[("nwc",)], slot="c1")
    P.dma("sp", sg_c[:, :], sg_d[:, :], w=[("sgc",)], slot="c2")
    P.dma("sp", su_c[:, :], su_d[:, :], w=[("suc",)], slot="c3")
    P.dma("sp", wstg[1][:, 0:128], eye_d[:, :], w=[("wstg", 1)], slot="ws1")
    P.op("act", lambda e: e.activation(out=eye_b[:, :], in_=wstg[1][:, 0:128], func=AF.Copy),
         r=[("wstg", 1)], w=[("eyeb",)])
    P.op("dve", lambda e: e.memset(ones_rf[:, :], 1.0), w=[("onesrf",)])
    P.dma("sp", wstg[0][0:1, 0:512], sdr_d[0:1, 0:512], r=[("wstg", 0)],
          w=[("wstg", 0), ("sdr", 0)], slot="ws0")

    for j in range(16):
        P.dma("sp", xn[j][:, :], x2n_d[j * 128:(j + 1) * 128, :], w=[("xn", j)],
              slot="xnload")

    for si in range(2):
        sp_f = SCRF[:, 0:1024]
        P.op("dve", lambda e, sp_f=sp_f: e.memset(sp_f, 0.0),
             r=SR8 if False else [("scrf", r) for r in range(2)], w=[("scrf", 0), ("scrf", 1)])
        def a_load(w, si=si):
            row0 = (si * 7 + w) * 128
            pa = w % 2
            aw = SCRF[:, 1024 + pa * 1024:2048 + pa * 1024]
            rk = [("scrf", 2 + 2 * pa), ("scrf", 3 + 2 * pa)]
            P.dma("sp", aw, pA_d[row0:row0 + 128, :], r=rk, w=rk, slot=f"paw{pa}")
            P.dma("sp", gw_s[:, pa * 8:pa * 8 + 8], pG_d[row0:row0 + 128, :],
                  r=[("gws", pa)], w=[("gws", pa)], slot=f"pgw{pa}")
            P.op("act", lambda e, pa=pa: e.activation(
                out=eg_s[:, pa * 8:pa * 8 + 8], in_=gw_s[:, pa * 8:pa * 8 + 8],
                func=AF.Exp),
                r=[("gws", pa)], w=[("egs", pa)])

        a_load(0)
        for w in range(7):
            pa = w % 2
            aw = SCRF[:, 1024 + pa * 1024:2048 + pa * 1024]
            rk = [("scrf", 2 + 2 * pa), ("scrf", 3 + 2 * pa)]
            if w + 1 < 7:
                a_load(w + 1)
            for pp in range(8):
                P.op("dve", lambda e, sp_f=sp_f, aw=aw, pp=pp, pa=pa: e.scalar_tensor_tensor(
                    sp_f[:, pp * 128:(pp + 1) * 128], sp_f[:, pp * 128:(pp + 1) * 128],
                    eg_s[:, pa * 8 + pp:pa * 8 + pp + 1], aw[:, pp * 128:(pp + 1) * 128],
                    ALU.mult, ALU.add),
                    r=[("scrf", 0), ("scrf", 1), ("egs", pa)] + rk,
                    w=[("scrf", 0), ("scrf", 1)])
        P.op("act", lambda e, si=si, sp_f=sp_f: e.activation(
            out=spre[si][:, :], in_=sp_f, func=AF.Copy),
            r=[("scrf", 0), ("scrf", 1)], w=[("spre", si)])

    oisb1 = SCRF[:, 4096:6144]
    qgst1 = SCRB[:, 12288:13312]
    OIH = lambda p: [("oih", p, h) for h in range(16)]

    def b1_buf(sp):
        if sp % 2 == 0:
            return oisb0[:, :], qgst0[:, :]
        return oisb1, qgst1

    def b1_load(sp):
        p = sp % 2
        t0 = sp * 128
        ob, qq = b1_buf(sp)
        ex = ([("scrf", r) for r in (8, 9, 10, 11, 12)] if p == 1 else [])
        P.dma("sp", ob, oi_d[t0:t0 + 128, :], r=OIH(p) + ex, w=OIH(p) + ex,
              slot=f"oild{p}")
        P.dma("sp", qq.rearrange("k (a d) -> k a d", d=128),
              qg_d[:, t0:t0 + 128].rearrange("(a k) d -> k a d", k=128),
              r=[("qg2", p)], w=[("qg2", p)], slot=f"qgld{p}")

    b1_load(0)
    for sp in range(16):
        si = sp // 8
        p = sp % 2
        t0 = sp * 128
        ob, qq = b1_buf(sp)
        if sp + 1 < 16:
            b1_load(sp + 1)
        for kdc in range(8):
            for h2 in range(2):
                hb = 64 * h2
                head = kdc * 2 + h2
                bank = pbig[head % 4]
                P.op("pe", lambda e, qq=qq, kdc=kdc, hb=hb, si=si, bank=bank: e.matmul(
                    bank[:, 0:128], qq[hb:hb + 64, kdc * 128:(kdc + 1) * 128],
                    spre[si][hb:hb + 64, kdc * 128:(kdc + 1) * 128],
                    start=True, stop=True),
                    r=[("qg2", p), ("spre", si)], w=[PS(bank)])
                P.op("dve", lambda e, ob=ob, head=head, bank=bank: e.tensor_add(
                    ob[:, head * 128:(head + 1) * 128],
                    ob[:, head * 128:(head + 1) * 128], bank[:, 0:128]),
                    r=[PS(bank), ("oih", p, head)], w=[("oih", p, head)])
        for head in range(16):
            sq = onst[head % 2]
            P.op("act", lambda e, ob=ob, head=head, sq=sq: e.activation(
                out=sq[:, 0:128], in_=ob[:, head * 128:(head + 1) * 128],
                func=AF.Square, accum_out=stat[:, 8 + head:9 + head]),
                r=[("oih", p, head)], w=[("onst", head % 2), ("statB", p, head)])
        P.op("act", lambda e: e.activation(out=stat[:, 24:40], in_=stat[:, 8:24],
                                           func=AF.Ln, scale=invdv_c, bias=epsg_c),
             r=[("statB", p, h) for h in range(16)] + [("statL",)], w=[("statL",)])
        P.op("act", lambda e: e.activation(out=stat[:, 24:40], in_=stat[:, 24:40],
                                           func=AF.Exp, scale=P.c(-0.5)),
             r=[("statL",)], w=[("statL",)])
        for head in range(16):
            P.op("dve", lambda e, ob=ob, head=head: e.tensor_scalar(
                ob[:, head * 128:(head + 1) * 128], ob[:, head * 128:(head + 1) * 128],
                stat[:, 24 + head:25 + head], None, ALU.mult),
                r=[("statL",), ("oih", p, head)], w=[("oih", p, head)])
        P.dma("pool", onrm[t0:t0 + 128, :], ob, r=OIH(p),
              w=[("onrmW", sp)] + OIH(p), slot=f"onrm{p}")
    P.op("sp", lambda e: e.nop(), r=OIH(0) + OIH(1) + [("qg2", 0), ("qg2", 1)],
         w=[("oisb", 0), ("scrf", 8), ("scrf", 9), ("scrf", 10), ("scrf", 11),
            ("scrf", 12)])

    def b2_wload(ig):
        h = ig % 2
        co = h * 256
        for j in range(8):
            ws = wstg[j % 2]
            P.dma("sp", ws[:, 0:512],
                  wgb_d[:, ig * 4096 + j * 512:ig * 4096 + (j + 1) * 512],
                  r=[("wstg", j % 2)], w=[("wstg", j % 2)], slot=f"ws{j % 2}")
            for k in range(2):
                hc = j * 2 + k
                P.op("act", lambda e, ws=ws, hc=hc, co=co, k=k: e.activation(
                    out=wgrp[hc][:, co:co + 256], in_=ws[:, k * 256:(k + 1) * 256],
                    func=AF.Copy),
                    r=[("wstg", j % 2)], w=[("wgrp", hc, h)])

    b2_wload(0)
    for ig in range(8):
        h = ig % 2
        co = h * 256
        dvg = ig // 2
        c2 = (ig % 2) * 256
        if ig + 1 < 8:
            b2_wload(ig + 1)
        for sp in range(16):
            t0 = sp * 128
            pg = pbig[sp % 2]
            for hc in range(16):
                P.op("pe", lambda e, pg=pg, hc=hc, t0=t0, co=co: e.matmul(
                    pg[:, 0:256], xn[hc][:, t0:t0 + 128], wgrp[hc][:, co:co + 256],
                    start=(hc == 0), stop=(hc == 15)),
                    r=[("wgrp", hc, h), ("xn", hc)], w=[PS(pg)])
            gs = gsil[sp % 2]
            P.op("act", lambda e, pg=pg, gs=gs: e.activation(
                out=gs[:, 0:256], in_=pg[:, 0:256], func=AF.Silu),
                r=[PS(pg)], w=[("gsil", sp % 2)])
            on = onst[sp % 2]
            P.dma("pool", on[:, 0:256],
                  onrm[t0:t0 + 128, dvg * 512 + c2:dvg * 512 + c2 + 256],
                  r=[("onrmW", sp), ("onst", sp % 2)], w=[("onst", sp % 2)],
                  slot=f"onld{sp % 2}")
            og = ogsp[sp % 2]
            P.op("dve", lambda e, og=og, on=on, gs=gs: e.tensor_mul(
                og[:, 0:256], on[:, 0:256], gs[:, 0:256]),
                r=[("onst", sp % 2), ("gsil", sp % 2)], w=[("ogsp", sp % 2)])
            for j in range(2):
                vdc = dvg * 4 + (ig % 2) * 2 + j
                P.op("pe", lambda e, og=og, j=j: e.transpose(
                    ptb[:, :], og[:, j * 128:(j + 1) * 128], eye_b[:, :]),
                    r=[("ogsp", sp % 2), ("eyeb",)], w=[PS(ptb)])
                P.op("dve", lambda e, vdc=vdc, t0=t0: e.tensor_copy(
                    qar[vdc][:, t0:t0 + 128], ptb[:, :]),
                    r=[PS(ptb), ("qar", vdc)], w=[("qar", vdc)])

    for sg4 in range(4):
        for og in range(4):
            for vdc in range(16):
                ws = wstg[vdc % 2]
                P.dma("sp", ws[:, :], wob_d[vdc * 128:(vdc + 1) * 128, og * 512:(og + 1) * 512],
                      r=[("wstg", vdc % 2)], w=[("wstg", vdc % 2)], slot=f"ws{vdc % 2}")
                P.op("dve", lambda e, ws=ws, vdc=vdc: e.tensor_scalar(
                    wgrp[vdc][:, :], ws[:, :], nw_c[:, 0:1], None, ALU.mult),
                    r=[("wstg", vdc % 2), ("nwc",)], w=[("wgrp", vdc)])
            for sl in range(4):
                sp = sg4 * 4 + sl
                t0 = sp * 128
                pw = pbig[sl % 2]
                for vdc in range(16):
                    P.op("pe", lambda e, pw=pw, vdc=vdc, t0=t0: e.matmul(
                        pw[:, :], qar[vdc][:, t0:t0 + 128], wgrp[vdc][:, :],
                        start=(vdc == 0), stop=(vdc == 15)),
                        r=[("wgrp", vdc), ("qar", vdc)], w=[PS(pw)])
                xs = x2st[sl % 2]
                P.dma("sp", xs[:, :], x2tm_d[t0:t0 + 128, og * 512:(og + 1) * 512],
                      r=[("x2st", sl % 2)], w=[("x2st", sl % 2)], slot=f"x2ld{sl % 2}")
                x3t = SCRF[:, sl * 2048 + og * 512:sl * 2048 + (og + 1) * 512]
                P.op("dve", lambda e, pw=pw, xs=xs, x3t=x3t: e.tensor_add(
                    x3t, pw[:, :], xs[:, :]),
                    r=[PS(pw), ("x2st", sl % 2)], w=[("scrf", sl * 4 + og)])
                P.dma("pool", y_o[t0:t0 + 128, og * 512:(og + 1) * 512], x3t,
                      r=[("scrf", sl * 4 + og)], w=[("yw", sp, og)],
                      slot=f"yi{(sp * 4 + og) % 8}")
        for sl in range(4):
            sp = sg4 * 4 + sl
            t0 = sp * 128
            x3sp = SCRF[:, sl * 2048:(sl + 1) * 2048]
            rk = [("scrf", sl * 4 + g) for g in range(4)]
            P.op("dve", lambda e, x3sp=x3sp: e.reduce_sum(stat[:, 0:1], x3sp, axis=AX.X),
                 r=rk, w=[("stat",)])
            P.op("act", lambda e, x3sp=x3sp: e.activation(
                out=x3nt[:, :], in_=x3sp, func=AF.Square, accum_out=stat[:, 1:2]),
                r=rk, w=[("oisb", 0), ("stat",)])
            P.op("dve", lambda e: e.tensor_scalar(stat[:, 2:3], stat[:, 0:1], 1.0 / D,
                                                  None, ALU.mult),
                 r=[("stat",)], w=[("stat",)])
            P.op("dve", lambda e: e.tensor_scalar(stat[:, 3:4], stat[:, 2:3], stat[:, 2:3],
                                                  None, ALU.mult),
                 r=[("stat",)], w=[("stat",)])
            P.op("dve", lambda e: e.scalar_tensor_tensor(
                stat[:, 4:5], stat[:, 1:2], 1.0 / D, stat[:, 3:4], ALU.mult, ALU.subtract),
                r=[("stat",)], w=[("stat",)])
            P.op("act", lambda e: e.activation(out=stat[:, 5:6], in_=stat[:, 4:5],
                                               func=AF.Sqrt, bias=epsl_c),
                 r=[("stat",)], w=[("stat",)])
            P.op("dve", lambda e: e.reciprocal(stat[:, 6:7], stat[:, 5:6]),
                 r=[("stat",)], w=[("stat",)])
            P.op("dve", lambda e, x3sp=x3sp: e.tensor_scalar(
                x3nt[:, :], x3sp, stat[:, 2:3], stat[:, 6:7], ALU.subtract, ALU.mult),
                r=rk + [("stat",)], w=[("oisb", 0)])
            for j in range(16):
                P.op("pe", lambda e, j=j: e.transpose(
                    ptb[:, :], x3nt[:, j * 128:(j + 1) * 128], eye_b[:, :]),
                    r=[("oisb", 0), ("eyeb",)], w=[PS(ptb)])
                P.op("dve", lambda e, j=j, t0=t0: e.tensor_copy(
                    xn[j][:, t0:t0 + 128], ptb[:, :]),
                    r=[PS(ptb), ("xn", j)], w=[("xn", j)])

    def ugrp(j):
        return SCRB[:, j * 512:(j + 1) * 512]

    def dtmp(j):
        return SCRF[:, 4096 + j * 512:4096 + (j + 1) * 512]

    def ffn_wload(isup, ig):
        h = ig % 2
        co = h * 256
        G = isup * 8 + ig
        for half in range(2):
            wi = wstgi[half]
            o0 = G * 4096 + half * 2048
            P.dma("sp", wi[:, 0:2048], wgs[:, o0:o0 + 2048],
                  r=[("wstg", half)], w=[("wstg", half)], slot=f"ws{half}")
            for k in range(8):
                hc = half * 8 + k
                P.op("act", lambda e, wi=wi, hc=hc, co=co, k=k: e.activation(
                    out=wgrp[hc][:, co:co + 256], in_=wi[:, k * 256:(k + 1) * 256],
                    func=AF.Copy, bias=-128.0),
                    r=[("wstg", half)], w=[("wgrp", hc, h)])
        for half in range(2):
            wi = wstgi[half]
            o0 = G * 4096 + half * 2048
            P.dma("sp", wi[:, 0:2048], wus[:, o0:o0 + 2048],
                  r=[("wstg", half)] + [("wgrp", half * 8 + k, h) for k in range(8)],
                  w=[("wstg", half)], slot=f"ws{half}")
            for k in range(8):
                hc = half * 8 + k
                trans = ([("scrf", hc // 2)] if isup == 0 and ig < 2 else [])
                P.op("act", lambda e, wi=wi, hc=hc, co=co, k=k: e.activation(
                    out=ugrp(hc)[:, co:co + 256], in_=wi[:, k * 256:(k + 1) * 256],
                    func=AF.Copy, bias=-128.0),
                    r=[("wstg", half)] + trans, w=[("ug", hc, h)] + trans)

    ffn_wload(0, 0)
    for isup in range(4):
        for ig in range(8):
            h = ig % 2
            co = h * 256
            if ig + 1 < 8:
                ffn_wload(isup, ig + 1)
            for ic2 in range(2):
                ic = isup * 16 + ig * 2 + ic2
                for tt in range(4):
                    pg = pbig[tt % 2]
                    pu = pbig[2 + tt % 2]
                    for hc in range(16):
                        P.op("pe", lambda e, pg=pg, hc=hc, co=co, ic2=ic2, tt=tt: e.matmul(
                            pg[:, :], wgrp[hc][:, co + ic2 * 128:co + (ic2 + 1) * 128],
                            xn[hc][:, tt * 512:(tt + 1) * 512],
                            start=(hc == 0), stop=(hc == 15)),
                            r=[("wgrp", hc, h), ("xn", hc)], w=[PS(pg)])
                        P.op("pe", lambda e, pu=pu, hc=hc, co=co, ic2=ic2, tt=tt: e.matmul(
                            pu[:, :], ugrp(hc)[:, co + ic2 * 128:co + (ic2 + 1) * 128],
                            xn[hc][:, tt * 512:(tt + 1) * 512],
                            start=(hc == 0), stop=(hc == 15)),
                            r=[("ug", hc, h), ("xn", hc)], w=[PS(pu)])
                    gs = gsil[tt % 2]
                    us = ogsp[tt % 2]
                    P.op("dve", lambda e, gs=gs, pg=pg, ic=ic: e.tensor_scalar(
                        gs[:, :], pg[:, :], sg_c[:, ic:ic + 1], None, ALU.mult),
                        r=[PS(pg), ("sgc",)], w=[("gsil", tt % 2)])
                    P.op("act", lambda e, gs=gs: e.activation(
                        out=gs[:, :], in_=gs[:, :], func=AF.Silu),
                        r=[("gsil", tt % 2)], w=[("gsil", tt % 2)])
                    P.op("dve", lambda e, us=us, pu=pu, ic=ic: e.tensor_scalar(
                        us[:, :], pu[:, :], su_c[:, ic:ic + 1], None, ALU.mult),
                        r=[PS(pu), ("suc",)], w=[("ogsp", tt % 2)])
                    hloc = ic % 16
                    P.op("dve", lambda e, gs=gs, us=us, hloc=hloc, tt=tt: e.tensor_mul(
                        qar[hloc][:, tt * 512:(tt + 1) * 512], gs[:, :], us[:, :]),
                        r=[("gsil", tt % 2), ("ogsp", tt % 2), ("qar", hloc)],
                        w=[("qar", hloc)])
        def down_wload(og2):
            use_u = og2 % 2 == 1
            dst = ugrp if use_u else (lambda j: wgrp[j][:, :])
            dk0 = (lambda j: ("ug", j, 0)) if use_u else (lambda j: ("wgrp", j, 0))
            dk1 = (lambda j: ("ug", j, 1)) if use_u else (lambda j: ("wgrp", j, 1))
            Gd = isup * 8 + og2
            for half in range(2):
                wi = wstgi[half]
                o0 = Gd * 4096 + half * 2048
                P.dma("sp", wi[:, 0:2048], wds[:, o0:o0 + 2048],
                      r=[("wstg", half)], w=[("wstg", half)], slot=f"ws{half}")
                for k in range(8):
                    icd = half * 8 + k
                    P.op("act", lambda e, wi=wi, icd=icd, k=k, dst=dst: e.activation(
                        out=dst(icd)[:, 0:256], in_=wi[:, k * 256:(k + 1) * 256],
                        func=AF.Copy, bias=-128.0),
                        r=[("wstg", half)], w=[dk0(icd), dk1(icd)])

        down_wload(0)
        for og2 in range(8):
            use_u = og2 % 2 == 1
            dst = ugrp if use_u else (lambda j: wgrp[j][:, :])
            dkey = (lambda j: ("ug", j, 0)) if use_u else (lambda j: ("wgrp", j, 0))
            ogq = og2 // 2
            P.dma("sp", wstg[0][0:1, 0:256], sdr_d[0:1, og2 * 256:(og2 + 1) * 256],
                  r=[("wstg", 0), ("sdr", 0)], w=[("wstg", 0), ("sdr", 0)],
                  slot="ws0")
            P.op("pe", lambda e: e.matmul(pbig[3][:, 0:256], ones_rf[:, :],
                                          wstg[0][0:1, 0:256], start=True, stop=True),
                 r=[("wstg", 0), ("sdr", 0), ("onesrf",)], w=[PS(pbig[3])])
            P.op("dve", lambda e: e.tensor_copy(sdbt[:, 0:256], pbig[3][:, 0:256]),
                 r=[PS(pbig[3])], w=[("onst", 0)])
            if og2 + 1 < 8:
                down_wload(og2 + 1)
            for sp in range(16):
                t0 = sp * 128
                pd = pbig[sp % 2]
                for icd in range(16):
                    P.op("pe", lambda e, pd=pd, icd=icd, t0=t0, dst=dst: e.matmul(
                        pd[:, 0:256], qar[icd][:, t0:t0 + 128], dst(icd)[:, 0:256],
                        start=(icd == 0), stop=(icd == 15)),
                        r=[dkey(icd), ("qar", icd)], w=[PS(pd)])
                dj = sp % 4
                P.op("dve", lambda e, pd=pd, dj=dj: e.tensor_mul(
                    dtmp(dj)[:, 0:256], pd[:, 0:256], sdbt[:, 0:256]),
                    r=[PS(pd), ("onst", 0)], w=[("scrf", 8 + dj)])
                P.dma("pool", y_o[t0:t0 + 128, og2 * 256:(og2 + 1) * 256],
                      dtmp(dj)[:, 0:256],
                      r=[("scrf", 8 + dj), ("yw", sp, og2 // 2), ("yd", sp, og2)],
                      w=[("yd", sp, og2)], slot=f"ya{(sp + og2) % 8}",
                      accum_op=ALU.add)
        if isup + 1 < 4:
            ffn_wload(isup + 1, 0)

    out_keys = [k for k in P.last_w if k and k[0] in ("yd", "yw")]
    P.op("sp", lambda e: e.nop(), r=out_keys)

    fc = P.setup_consts(nc, ctx)
    sems = P.finalize(nc, ctx)
    ctx.close()
    return nc, fc


def stage_L34(st, l2res):
    out3 = []
    for c in range(8):
        out3.append({"x2n": np.asarray(l2res[c]["x2n"])})
    return out3


def stage_L4(st, l2res, l3res):
    out4 = []
    for b in range(2):
        A_abs = [None] * 8
        G_abs = [None] * 8
        for p in range(4):
            c = 4 * b + p
            acl = np.asarray(l3res[c]["acl"], np.float32)
            gcl = np.asarray(l3res[c]["gcl"], np.float32)
            A_abs[p] = acl[0:128]
            A_abs[7 - p] = acl[128:256]
            G_abs[p] = gcl[0:128]
            G_abs[7 - p] = gcl[128:256]
        for p in range(4):
            c = 4 * b + p
            pA = np.zeros((14, 128, 1024), np.float32)
            pG = np.zeros((14, 128, 8), np.float32)
            for si, m in enumerate((p, 7 - p)):
                for w in range(m):
                    pA[si * 7 + w] = A_abs[w]
                    pG[si * 7 + w] = G_abs[w]
            d = {
                "oi": np.asarray(l3res[c]["oi"]),
                "qg": np.asarray(l3res[c]["qg"]),
                "pA": np.ascontiguousarray(pA.reshape(1792, 1024)),
                "pG": np.ascontiguousarray(pG.reshape(1792, 8)),
                "x2n": np.asarray(l2res[c]["x2n"]),
                "x2tm": np.ascontiguousarray(np.asarray(l2res[c]["x2"], np.float32).T),
            }
            if len(out4) <= c:
                out4.extend([None] * (c + 1 - len(out4)))
            out4[c] = d
    return out4


def run_L3(st, l3in, micro=99):
    nc, fc = build_L3(micro)
    in_maps = []
    for c in range(8):
        d = dict(l3in[c])
        d.update({"wqb": st["wqb"], "wkb": st["wkb"], "wvb": st["wvb"],
                  "wgk1": st["wgk1"], "wgk2": st["wgk2"], "bgk2": st["bgk2"],
                  "tri": st["tri"], "smask": st["smask"], "eye": st["eye"],
                  "fconsts": fc})
        in_maps.append(d)
    res = run_bass_kernel_spmd(nc, in_maps, core_ids=list(range(8)))
    return res.results


def run_L4(st, l4in):
    nc, fc = build_L4()
    in_maps = []
    for c in range(8):
        d = dict(l4in[c])
        d.update({"wgb": st["wgbs"], "wob": st["wob"], "nw": st["nw"],
                  "wgs": st["wgs"], "wus": st["wus"], "wds": st["wds"],
                  "sg": st["sg"], "su": st["su"], "sdr": st["sdr"],
                  "eye": st["eye"], "fconsts": fc})
        in_maps.append(d)
    res = run_bass_kernel_spmd(nc, in_maps, core_ids=list(range(8)))
    return res.results


def kernel(**inputs):
    st = host_stage(inputs)
    l1res = run_L1(st)
    l2in = stage_L2(st, l1res)
    l2res = run_L2(st, l2in)
    l3in = stage_L34(st, l2res)
    l3res = run_L3(st, l3in)
    l4in = stage_L4(st, l2res, l3res)
    l4res = run_L4(st, l4in)
    y = np.zeros((B, S, D), np.float32)
    for c in range(8):
        b, p = divmod(c, 4)
        rows = _own_rows(p)
        y[b, rows, :] = np.asarray(l4res[c]["y"], np.float32)
    return y
